# version 09 (final)
# baseline (speedup 1.0000x reference)
import numpy as np
from contextlib import ExitStack

import concourse.bass as bass
import concourse.mybir as mybir
from concourse.bass_utils import run_bass_kernel_spmd

F32 = mybir.dt.float32
BF16 = mybir.dt.bfloat16
AF = mybir.ActivationFunctionType
ALU = mybir.AluOpType
AX = mybir.AxisListType

B, L, E, Hh, Dd = 2, 8192, 1024, 16, 64
INNER = 4096
TOK = 2048
TC = TOK // 128
FC = E // 128
IC = INNER // 128
NCORE = 8
CROUND = float(2**23 + 2**22)
EPS_LN = 1e-5
EPS_DEN = 1e-6
HTOK = TOK + 4

AG2_WG = 128 * INNER
AG2_WU = 128 * INNER
AG2_WD = 512 * E
AG2_W2 = 512 * E
AG2_W1 = 128 * INNER
AG2_PER = AG2_WG + AG2_WU + AG2_WD + AG2_W2 + AG2_W1
AG2_OFF = {"wg": 0, "wu": AG2_WG, "wd": AG2_WG + AG2_WU,
           "w2": AG2_WG + AG2_WU + AG2_WD, "w1": AG2_WG + AG2_WU + AG2_WD + AG2_W2}


class Bld:
    def __init__(self, nc, es):
        self.nc = nc
        self.blk = es.enter_context(nc.Block())
        self.sem = {}
        self.cnt = {}
        for e in ["tensor", "vector", "scalar", "gpsimd", "sync",
                  "dma", "dmas", "dmag", "fence"]:
            self.sem[e] = es.enter_context(nc.semaphore("sem_" + e))
            self.cnt[e] = 0
        self.chain = None
        self.tokbuf = None

    FENCE_OPS = {"tensor_reduce", "reduce_max", "reduce_sum", "reciprocal"}

    class _Rec:
        def __init__(self, eng, bld):
            self._eng = eng
            self._bld = bld
            self._last = None
            self._fenced = None
            self._fence_tok = None
            self._dirty = set()

        @staticmethod
        def _apname(o):
            t = getattr(o, "tensor", None)
            return getattr(t, "name", None) if t is not None else None

        def _fence_now(self, ins):
            b = self._bld
            b.cnt["fence"] += 1
            ins.then_inc(b.sem["fence"], 1)
            self._eng.wait_ge(b.sem["fence"], b.cnt["fence"])
            self._fenced = ins
            self._fence_tok = ("fence", b.cnt["fence"])
            self._last = None
            self._dirty.clear()

        def __getattr__(self, n):
            a = getattr(self._eng, n)
            if callable(a):
                if n.startswith("dma") or n in ("wait_ge", "nop",
                                                "collective_compute"):
                    return a

                def w(*args, **kw):
                    ins_n = set()
                    outs_n = set()
                    for i, o in enumerate(args):
                        nm = Bld._Rec._apname(o)
                        if nm:
                            (outs_n if i == 0 else ins_n).add(nm)
                    for k, o in kw.items():
                        nm = Bld._Rec._apname(o)
                        if nm:
                            (outs_n if k in ("out", "accum_out")
                             else ins_n).add(nm)
                    if (self._dirty & ins_n) and self._last is not None:
                        self._fence_now(self._last)
                    r = a(*args, **kw)
                    if hasattr(r, "then_inc"):
                        self._last = r
                        self._dirty |= outs_n
                        if n in Bld.FENCE_OPS or "accum_out" in kw:
                            self._fence_now(r)
                    return r
                return w
            return a

    def task(self, engine, emit, waits=(), dma=None, chain=True):
        sem_name = dma if dma else engine
        waits = list(waits)
        if chain and self.chain is not None:
            waits.append(self.chain)
        holder = {}

        def f(eng):
            rec = Bld._Rec(eng, self)
            for (s, v) in waits:
                if v > 0:
                    eng.wait_ge(self.sem[s], v)
            r = emit(rec)
            if dma:
                ins, n = r
                self.cnt[sem_name] += 16 * n
            else:
                ins = r if (r is not None and hasattr(r, "then_inc"))                     else rec._last
                if ins is not None and ins is getattr(rec, "_fenced", None):
                    ins = None
                if ins is None and rec._fence_tok is not None:
                    holder["tok"] = rec._fence_tok
                    return
                if ins is None:
                    ins = eng.nop()
                self.cnt[sem_name] += 1
                ins.then_inc(self.sem[sem_name], 1)
            holder["tok"] = (sem_name, self.cnt[sem_name])

        getattr(self.blk, engine)(f)
        tok = holder["tok"]
        if chain:
            self.chain = tok
        return tok

    def dmas(self, engine, sem_name, pairs, waits=(), chain=True, slow=False):
        def emit(eng):
            last = None
            for (d, s) in pairs:
                last = eng.dma_start(out=d, in_=s,
                                     allow_slow_non_contiguous=slow)
                last.then_inc(self.sem[sem_name], 16)
            return (last, len(pairs))
        return self.task(engine, emit, waits, dma=sem_name, chain=chain)


def build(nc: bass.Bass, DBG=None, stop=99, skip_prep=False):
    DBG = DBG or {}
    dt = nc.dram_tensor
    x_tm = dt("x_tm", [TOK, E], F32, kind="ExternalInput")
    x_fmT = dt("x_fmT", [E, TOK], F32, kind="ExternalInput")
    x_haloT = dt("x_haloT", [E, 4], F32, kind="ExternalInput")
    x_halo = dt("x_halo", [4, E], F32, kind="ExternalInput")
    WQ = dt("Wq", [E, E], F32, kind="ExternalInput")
    WK = dt("Wk", [E, E], F32, kind="ExternalInput")
    WV = dt("Wv", [E, E], F32, kind="ExternalInput")
    WPQ = dt("Wphi_q", [E, E], F32, kind="ExternalInput")
    WPK = dt("Wphi_k", [E, E], F32, kind="ExternalInput")
    w2_slice = dt("w2_slice", [512, E], F32, kind="ExternalInput")
    wg_slice = dt("wg_slice", [128, INNER], F32, kind="ExternalInput")
    wu_slice = dt("wu_slice", [128, INNER], F32, kind="ExternalInput")
    wd_slice = dt("wd_slice", [512, E], F32, kind="ExternalInput")
    w1_slice = dt("w1_slice", [128, INNER], F32, kind="ExternalInput")
    ga_slice = dt("ga_slice", [128, 1], F32, kind="ExternalInput")
    convw_bd = dt("convw_bd", [FC, 5, 128, 128], F32, kind="ExternalInput")
    VEC_FM = ["bq_fm", "bk_fm", "gq_fm", "Bq_fm", "gk_fm", "Bk_fm",
              "ga_fm", "Ba_fm", "bphik_fm", "convb_fm"]
    vecs_fm = {nm: dt(nm, [128, FC], F32, kind="ExternalInput") for nm in VEC_FM}
    b1_fm = dt("b1_fm", [128, IC], F32, kind="ExternalInput")
    ROWS = ["bq_row", "bk_row", "bv_row", "bphiq_row", "bphik_row",
            "b2_row", "Ba_row"]
    rows_in = {nm: dt(nm, [1, E], F32, kind="ExternalInput") for nm in ROWS}
    gvB_in = dt("gvB", [128, E], F32, kind="ExternalInput")
    bvB_in = dt("bvB", [128, E], F32, kind="ExternalInput")
    gaB_in = dt("gaB", [128, E], F32, kind="ExternalInput")
    gammaB_in = dt("gammaB", [128, 1], F32, kind="ExternalInput")
    cmask_in = dt("cmask", [128, 4], F32, kind="ExternalInput")
    zpad_in = dt("zpad", [16, 1], F32, kind="ExternalInput")
    ident_in = dt("ident_in", [128, 128], BF16, kind="ExternalInput")
    R16_in = dt("R16_in", [16, E], BF16, kind="ExternalInput")
    onesrow_in = dt("onesrow_in", [1, HTOK], BF16, kind="ExternalInput")
    onescol_in = dt("onescol_in", [128, 1], BF16, kind="ExternalInput")
    ones16_in = dt("ones16_in", [1, 16], BF16, kind="ExternalInput")
    out = dt("out", [TOK, E], F32, kind="ExternalOutput")

    ag1_in = dt("ag1_in", [4], F32)
    ag1_out = dt("ag1_out", [4], F32)
    ag2_in = dt("ag2_in", [AG2_PER], BF16)
    ag2_out = dt("ag2_out", [NCORE, AG2_PER], BF16)
    ag3_in = dt("ag3_in", [E], F32)
    ag3_out = dt("ag3_out", [E], F32)
    ag4_in = dt("ag4_in", [32], F32)
    ag4_out = dt("ag4_out", [4, 32], F32)
    ag5_in = dt("ag5_in", [65 * 1024], F32)
    ag5_out = dt("ag5_out", [65 * 1024], F32)
    sp_outacc = dt("sp_outacc", [TOK, E], F32)
    sp_yk = dt("sp_yk", [FC, 128, TOK], BF16)
    sp_tvq = dt("sp_tvq", [FC, 128, HTOK], BF16)
    sp_tvk = dt("sp_tvk", [TC, 128, E], BF16)
    sp_vn = dt("sp_vn", [TC, 128, E], BF16)
    sp_xq = dt("sp_xq", [FC, 128, TOK], BF16)
    sp_guq = dt("sp_guq", [TC, 128, IC, 128], BF16)
    scal1 = dt("scal1", [4], F32)

    def ag2v(r, part):
        a = ag2_out.ap()[r]
        off = AG2_OFF[part]
        p = 512 if part in ("wd", "w2") else 128
        n = E if part in ("wd", "w2") else INNER
        return a[off:off + p * n].rearrange("(p n) -> p n", p=p)

    dbg_t = {nm: dt("dbg_" + nm, shp, F32, kind="ExternalOutput")
             for nm, shp in DBG.items()}

    es = ExitStack()
    SB = lambda n, shp, d=F32, **kw: es.enter_context(nc.sbuf_tensor(n, shp, d, **kw))
    PS = lambda n, shp: es.enter_context(nc.psum_tensor(n, shp, F32))

    with es:
        K = Bld(nc, es)
        K.tokbuf = SB("tokbuf", [1, 8], F32)

        ident = SB("ident", [128, 128], BF16)
        R16 = SB("R16", [16, E], BF16)
        onesrow = SB("onesrow", [1, HTOK], BF16)
        onescol = SB("onescol", [128, 1], BF16)
        ones16 = SB("ones16", [1, 16], BF16)
        c_cmask = SB("c_cmask", [128, 4], F32)
        c_zpad = SB("c_zpad", [16, 1], F32)
        c_gammaB = SB("c_gammaB", [128, 1], F32)
        c_gasl = SB("c_gasl", [128, 1], F32)
        v_fm = {nm: SB("v_" + nm, [128, FC], F32) for nm in VEC_FM}
        v_b1 = SB("v_b1", [128, IC], F32)
        v_b1p = SB("v_b1p", [128, IC], F32)
        c_gvB = SB("c_gvB", [128, E], BF16)
        c_bvB = SB("c_bvB", [128, E], BF16)
        c_gaI = SB("c_gaI", [128, E], BF16)
        c_gaB32 = SB("c_gaB32", [128, E], F32)
        st_mu = SB("st_mu", [128, TC], F32)
        st_rs = SB("st_rs", [128, TC], F32)
        st_s = SB("st_s", [128, TC], F32)
        st_r = SB("st_r", [128, TC], F32)
        st_rsk_tm = SB("st_rsk_tm", [128, TC], F32)
        lrow = SB("lrow", [1, 2080], F32)
        lrow2 = SB("lrow2", [1, 2080], F32)
        rowC = SB("rowC", [16, 2080], F32)
        rowD = SB("rowD", [16, 2080], F32)
        hm16 = SB("hm16", [16, 2080], BF16)
        rx2t = SB("rx2t", [2, 2080], BF16)
        y2 = SB("y2", [2, 2080], BF16)
        r_rsY = SB("r_rsY", [1, 2080], BF16)
        r_mkrk = SB("r_mkrk", [1, 2080], BF16)
        rx2h = SB("rx2h", [2, 4], BF16)
        rq2h = SB("rq2h", [2, 4], BF16)
        r_rsqh_bf = SB("r_rsqh_bf", [1, 4], BF16)
        exS = SB("exS", [2, E], BF16)
        exM = SB("exM", [2, E], BF16)
        rowtmp2 = SB("rowtmp2", [1, E], F32)
        rowtmp_bf = SB("rowtmp_bf", [1, E], BF16)
        b2pBa_bf = SB("b2pBa_bf", [1, E], BF16)
        bog = SB("bog", [128, FC], BF16)
        stat48 = SB("stat48", [128, 3 * TC], BF16)
        qacc = SB("qacc", [128, FC], F32)
        s0row = SB("s0row", [1, 4], F32)
        smallf = SB("smallf", [128, 16], F32)
        junk = SB("junk", [128, TOK], F32)
        junkb = SB("junkb", [128, E], BF16)
        sc_Mi = SB("sc_Mi", [16, 1], F32)
        sc_negM = SB("sc_negM", [16, 1], F32)
        sc_Zi = SB("sc_Zi", [16, 1], F32)
        sc_t16 = SB("sc_t16", [16, 16], F32)
        sc_Mall = SB("sc_Mall", [16, 4], F32)
        sc_Zall = SB("sc_Zall", [16, 4], F32)
        sc_Mc = SB("sc_Mc", [16, 1], F32)
        sc_Zc = SB("sc_Zc", [16, 1], F32)
        sc_fac = SB("sc_fac", [16, 1], F32)
        sc_a0 = SB("sc_a0", [16, 1], F32)
        a0_bf = SB("a0_bf", [16, 1], BF16)
        qg_fm = SB("qg_fm", [128, FC], F32)
        qgp_pad = SB("qgp_pad", [128, 16 * FC], BF16)
        qgb_pad = SB("qgb_pad", [128, 16 * FC], BF16)
        ks_pad = SB("ks_pad", [128, 16 * FC], BF16)
        r_hsum = SB("r_hsum", [1, 16], BF16)
        r_qgs = SB("r_qgs", [1, 16], BF16)
        co16 = SB("co16", [16, 1], F32)
        qgs16 = SB("qgs16", [16, 1], F32)
        hs16 = SB("hs16", [16, 1], F32)
        ksum_fm = SB("ksum_fm", [128, FC], F32)
        pk0_fm = SB("pk0_fm", [128, FC], F32)
        kvbd = SB("kvbd", [128, FC, 128], BF16)
        p_sw = SB("p_sw", [128, 8], F32)
        rV = SB("rV", [128, 4], F32)
        rV2 = SB("rV2", [128, 4], F32)
        p_absn = SB("p_absn", [128, 4], F32)
        agv = SB("agv", [128, 4], F32)
        gcb_fm = SB("gcb_fm", [128, FC], F32)
        b1r_fm = SB("b1r_fm", [128, IC], F32)
        WST = SB("WST", [128, 2, INNER], BF16)
        WFLAT = WST.ap().rearrange("p a n -> p (a n)")
        WARE = WFLAT.rearrange("p (a n) -> p a n", a=FC)
        stgA = SB("stgA", [128, 2, E], BF16)
        stgAF = stgA.ap().rearrange("p a n -> p (a n)")
        stgB = SB("stgB", [128, 2, E], BF16)
        stgBF = stgB.ap().rearrange("p a n -> p (a n)")
        stgC = SB("stgC", [128, 1, E], F32)
        fout = SB("fout", [128, E], F32)
        xq_fmt = SB("xq_fmt", [128, FC, 128], BF16, is_transpose_dest=True)

        pBig = PS("pBig", [128, 2048])
        pA = PS("pA", [128, 512])
        pB = PS("pB", [128, 512])
        pC = PS("pC", [128, 512])
        pD = PS("pD", [128, 512])

        pairs = [
            (ident[:, :], ident_in.ap()), (R16[:, :], R16_in.ap()),
            (onesrow[:, :], onesrow_in.ap()), (onescol[:, :], onescol_in.ap()),
            (ones16[:, :], ones16_in.ap()), (c_cmask[:, :], cmask_in.ap()),
            (c_zpad[:, :], zpad_in.ap()),
            (c_gammaB[:, :], gammaB_in.ap()), (c_gasl[:, :], ga_slice.ap()),
            (v_b1[:, :], b1_fm.ap()), (c_gaB32[:, :], gaB_in.ap()),
        ]
        pairs += [(v_fm[nm][:, :], vecs_fm[nm].ap()) for nm in VEC_FM]
        K.dmas("sync", "dma", pairs)
        K.dmas("gpsimd", "dmag", [(c_gvB[:, :], gvB_in.ap()),
                                  (c_bvB[:, :], bvB_in.ap())])


        def _dump(nm, src_ap, engine="sync"):
            if nm in DBG:
                q = "dma" if engine == "sync" else "dmag"
                K.dmas(engine, q, [(dbg_t[nm].ap(), src_ap)])

        def _dummy_fin():
            K.dmas("sync", "dma", [(out.ap()[0:128, :], fout[:, :])])
            K.task("sync", lambda q: None)
        if stop <= 1:
            _dummy_fin()
            return nc
        stg32_g = nc.sbuf_tensor("stg32", [128, 2, INNER], F32)
        stg32 = stg32_g.__enter__()
        if not skip_prep:
            K.dmas("sync", "dma", [(stg32[:, 0, :], wg_slice.ap()),
                                   (stg32[:, 1, :], wu_slice.ap())])

            def abs1(v):
                v.tensor_reduce(p_absn[:, 0:1], stg32[:, 0, :], AX.X, ALU.add,
                                apply_absolute_value=True)
                return v.tensor_reduce(p_absn[:, 1:2], stg32[:, 1, :], AX.X, ALU.add,
                                       apply_absolute_value=True)
            K.task("vector", abs1)
            K.dmas("sync", "dma",
                   [(stg32[:, 0, i * E:(i + 1) * E],
                     wd_slice.ap()[128 * i:128 * (i + 1), :]) for i in range(4)])
            K.task("vector", lambda v: v.tensor_reduce(
                p_absn[:, 2:3], stg32[:, 0, :], AX.X, ALU.add,
                apply_absolute_value=True))
            K.dmas("sync", "dma",
                   [(lrow[0:1, 128 * k:128 * (k + 1)], p_absn[:, k:k + 1])
                    for k in range(3)], slow=True)

            def absum(v):
                for k in range(3):
                    v.tensor_reduce(lrow[0:1, 512 + k:513 + k],
                                    lrow[0:1, 128 * k:128 * (k + 1)], AX.X, ALU.add)
                return None
            K.task("vector", absum)
            K.dmas("gpsimd", "dmag", [(ag1_in.ap()[0:3], lrow[0:1, 512:515])])

            def ag1_go(g):
                return g.collective_compute(
                    "AllReduce", ALU.add, replica_groups=[list(range(NCORE))],
                    ins=[ag1_in.ap().opt()], outs=[ag1_out.ap().opt()])
            K.task("gpsimd", ag1_go)
            K.dmas("gpsimd", "dmag",
                   [(agv[:, 0:4],
                     ag1_out.ap().rearrange("(a n) -> a n", a=1)
                     .broadcast_to((128, 4)))])

            def sw_calc(v):
                v.tensor_scalar(p_sw[:, 0:4], agv[:, 0:4], 1.0 / (E * INNER), 1e-5,
                                ALU.mult, ALU.max)
                v.reciprocal(rV[:, :], p_sw[:, 0:4])
                return v.tensor_copy(p_sw[:, 4:8], rV[:, :])
            K.task("vector", sw_calc)

            K.dmas("sync", "dma", [(stg32[:, 0, :], wg_slice.ap()),
                                   (stg32[:, 1, :], wu_slice.ap())])

            def tern_gu(v):
                for i in range(2):
                    v.tensor_scalar(stg32[:, i, :], stg32[:, i, :],
                                    p_sw[:, 4 + i:5 + i], CROUND, ALU.mult, ALU.add)
                    v.tensor_scalar(stg32[:, i, :], stg32[:, i, :], -CROUND, None,
                                    ALU.add)
                    v.tensor_scalar(WST[:, i, :], stg32[:, i, :], -1.0, 1.0,
                                    ALU.max, ALU.min)
                return None
            K.task("vector", tern_gu)
            K.dmas("sync", "dma", [
                (ag2_in.ap()[AG2_OFF["wg"]:AG2_OFF["wg"] + AG2_WG]
                 .rearrange("(p n) -> p n", p=128), WST[:, 0, :]),
                (ag2_in.ap()[AG2_OFF["wu"]:AG2_OFF["wu"] + AG2_WU]
                 .rearrange("(p n) -> p n", p=128), WST[:, 1, :]),
            ])
            K.dmas("sync", "dma",
                   [(stg32[:, 0, i * E:(i + 1) * E],
                     wd_slice.ap()[128 * i:128 * (i + 1), :]) for i in range(4)])

            def tern_wd(v):
                v.tensor_scalar(stg32[:, 0, :], stg32[:, 0, :], p_sw[:, 6:7],
                                CROUND, ALU.mult, ALU.add)
                v.tensor_scalar(stg32[:, 0, :], stg32[:, 0, :], -CROUND, None,
                                ALU.add)
                return v.tensor_scalar(WST[:, 0, :], stg32[:, 0, :], -1.0, 1.0,
                                       ALU.max, ALU.min)
            K.task("vector", tern_wd)
            K.dmas("sync", "dma", [
                (ag2_in.ap()[AG2_OFF["wd"]:AG2_OFF["wd"] + AG2_WD]
                 .rearrange("(a p n) -> p a n", p=128, a=4),
                 WST[:, 0, :].rearrange("p (a n) -> p a n", a=4)),
            ])
            K.dmas("gpsimd", "dmag", [
                (ag2_in.ap()[AG2_OFF["w2"]:AG2_OFF["w2"] + AG2_W2],
                 w2_slice.ap().rearrange("p n -> (p n)")),
            ])
            K.dmas("sync", "dma", [(stg32[:, 0, :], w1_slice.ap())])
            K.task("vector", lambda v: v.tensor_scalar(
                WST[:, 0, :], stg32[:, 0, :], c_gasl[:, 0:1], None, ALU.mult))
            K.dmas("sync", "dma", [
                (ag2_in.ap()[AG2_OFF["w1"]:AG2_OFF["w1"] + AG2_W1]
                 .rearrange("(p n) -> p n", p=128), WST[:, 0, :]),
            ])

            if stop <= 2:
                stg32_g.__exit__(None, None, None)
                _dummy_fin()
                return nc

            def ag2_go(g):
                return g.collective_compute(
                    "AllGather", ALU.bypass, replica_groups=[list(range(NCORE))],
                    ins=[ag2_in.ap().opt()], outs=[ag2_out.ap().opt()])
            t_ag2 = K.task("gpsimd", ag2_go)
        else:
            t_ag2 = ("gpsimd", 0)
        if stop <= 3:
            stg32_g.__exit__(None, None, None)
            _dummy_fin()
            return nc
        xdb = stg32.ap().rearrange("p a n -> p (a n)").rearrange(
            "p (a n) -> p a n", a=8)
        for c in range(TC):
            K.dmas("sync", "dma",
                   [(xdb[:, c % 2, :], x_tm.ap()[128 * c:128 * (c + 1), :])])

            def stats1(v, c=c):
                xc = xdb[:, c % 2, :]
                v.tensor_reduce(rV[:, 0:1], xc, AX.X, ALU.add)
                v.tensor_scalar(st_mu[:, c:c + 1], rV[:, 0:1], 1.0 / E, None,
                                ALU.mult)
                v.tensor_tensor(junk[:, 0:E], xc, xc, ALU.mult)
                v.tensor_reduce(rV[:, 1:2], junk[:, 0:E], AX.X, ALU.add)
                v.tensor_scalar(rV[:, 1:2], rV[:, 1:2], 1.0 / E, None, ALU.mult)
                v.tensor_tensor(rV[:, 2:3], st_mu[:, c:c + 1],
                                st_mu[:, c:c + 1], ALU.mult)
                v.tensor_tensor(rV[:, 1:2], rV[:, 1:2], rV[:, 2:3],
                                ALU.subtract)
                return v.tensor_scalar(st_rs[:, c:c + 1], rV[:, 1:2], EPS_LN,
                                       None, ALU.add)
            K.task("vector", stats1)
        K.task("scalar", lambda s: s.activation(st_rs[:, :], st_rs[:, :],
                                                AF.Sqrt))

        def rs_fin(v):
            v.reciprocal(junk[:, 0:TC], st_rs[:, :])
            return v.tensor_copy(st_rs[:, :], junk[:, 0:TC])
        K.task("vector", rs_fin)
        for c in range(TC):
            K.dmas("sync", "dma",
                   [(xdb[:, c % 2, :], x_tm.ap()[128 * c:128 * (c + 1), :])])

            def stats3(v, c=c):
                xc = xdb[:, c % 2, :]
                v.tensor_scalar(junk[:, 0:E], xc, st_mu[:, c:c + 1], None,
                                ALU.subtract)
                v.tensor_reduce(rV[:, 0:1], junk[:, 0:E], AX.X, ALU.max,
                                apply_absolute_value=True)
                v.tensor_tensor(rV[:, 0:1], rV[:, 0:1], st_rs[:, c:c + 1],
                                ALU.mult)
                v.tensor_scalar(st_r[:, c:c + 1], rV[:, 0:1], 1e-5,
                                1.0 / 127.0, ALU.max, ALU.mult)
                v.tensor_copy(rV[:, 1:2], st_r[:, c:c + 1])
                v.reciprocal(rV2[:, 0:1], rV[:, 1:2])
                v.tensor_copy(st_s[:, c:c + 1], rV2[:, 0:1])
                v.tensor_scalar(rV[:, 2:3], st_rs[:, c:c + 1], 1.0, None,
                                ALU.add)
                v.tensor_tensor(rV[:, 3:4], st_mu[:, c:c + 1],
                                st_rs[:, c:c + 1], ALU.mult)
                v.tensor_scalar(rV[:, 3:4], rV[:, 3:4], -1.0, None, ALU.mult)
                v.tensor_scalar(stgC[:, 0, :], xc, rV[:, 2:3], rV[:, 3:4],
                                ALU.mult, ALU.add)
                v.tensor_tensor(rV2[:, 1:2], st_rs[:, c:c + 1],
                                rV2[:, 0:1], ALU.mult)
                v.tensor_tensor(rV2[:, 2:3], rV[:, 3:4], rV2[:, 0:1], ALU.mult)
                v.tensor_scalar(junk[:, 0:E], xc, rV2[:, 1:2], rV2[:, 2:3],
                                ALU.mult, ALU.add)
                v.tensor_scalar(junk[:, 0:E], junk[:, 0:E], CROUND, -CROUND,
                                ALU.add, ALU.add)
                return v.tensor_copy(junkb[:, :], junk[:, 0:E])
            K.task("vector", stats3)
            K.dmas("sync", "dma",
                   [(sp_outacc.ap()[128 * c:128 * (c + 1), :], stgC[:, 0, :])])
            K.task("scalar", lambda s, c=c:
                   (s.dma_start_transpose(xq_fmt[:, :, :], junkb[:, :])
                    .then_inc(K.sem["dmas"], 16), 1), dma="dmas")
            K.dmas("sync", "dma",
                   [(sp_xq.ap()[j, :, 128 * c:128 * (c + 1)], xq_fmt[:, j, :])
                    for j in range(FC)])

        def pack48(v):
            v.tensor_scalar(stat48[:, 0:TC], st_mu[:, :], -1.0, None, ALU.mult)
            v.reciprocal(junk[:, 0:TC], st_rs[:, :])
            v.tensor_copy(stat48[:, TC:2 * TC], junk[:, 0:TC])
            return v.tensor_copy(stat48[:, 2 * TC:3 * TC], st_rs[:, :])
        K.task("vector", pack48)
        K.dmas("sync", "dma",
               [(rx2t[0:1, 128 * c:128 * (c + 1)], stat48[:, c:c + 1])
                for c in range(TC)] +
               [(rx2t[1:2, 128 * c:128 * (c + 1)], stat48[:, TC + c:TC + c + 1])
                for c in range(TC)])

        K.dmas("sync", "dma", [(xdb[0:4, 4, :], x_halo.ap())])

        def hstats(v):
            xh = xdb[0:4, 4, :]
            v.tensor_scalar(junk[0:4, 0:E], xh, 1.0, 0.0, ALU.mult, ALU.add,
                            accum_out=smallf[0:4, 0:1])
            v.tensor_scalar(smallf[0:4, 0:1], smallf[0:4, 0:1], 1.0 / E, None,
                            ALU.mult)
            v.scalar_tensor_tensor(junk[0:4, 0:E], xh, 1.0, xh, ALU.mult,
                                   ALU.mult, accum_out=smallf[0:4, 1:2])
            v.tensor_scalar(smallf[0:4, 1:2], smallf[0:4, 1:2], 1.0 / E, None,
                            ALU.mult)
            v.tensor_tensor(smallf[0:4, 2:3], smallf[0:4, 0:1],
                            smallf[0:4, 0:1], ALU.mult)
            v.tensor_tensor(smallf[0:4, 1:2], smallf[0:4, 1:2],
                            smallf[0:4, 2:3], ALU.subtract)
            return v.tensor_scalar(smallf[0:4, 1:2], smallf[0:4, 1:2], EPS_LN,
                                   None, ALU.add)
        K.task("vector", hstats)
        K.task("scalar", lambda s: s.activation(smallf[0:4, 1:2],
                                                smallf[0:4, 1:2], AF.Sqrt))

        def hstats2(v):
            v.reciprocal(smallf[0:4, 2:3], smallf[0:4, 1:2])
            v.tensor_scalar(smallf[0:4, 3:4], smallf[0:4, 0:1], -1.0, None,
                            ALU.mult)
            return None
        K.task("vector", hstats2)
        K.dmas("gpsimd", "dmag", [
            (rx2h[0:1, :].rearrange("a (c p) -> a p c", p=4), smallf[0:4, 3:4]),
            (rx2h[1:2, :].rearrange("a (c p) -> a p c", p=4), smallf[0:4, 1:2]),
        ], slow=True)
        _dump("outacc", sp_outacc.ap())
        _dump("xq0", sp_xq.ap()[0], "gpsimd")
        _dump("stmu", st_mu[:, :])
        _dump("strs", st_rs[:, :])
        _dump("st48", stat48[:, :], "gpsimd")
        _dump("rx2d", rx2t[0:2, :], "gpsimd")
        stg32_g.__exit__(None, None, None)

        if stop <= 4:
            _dummy_fin()
            return nc
        with nc.sbuf_tensor("xfm", [128, FC, HTOK], BF16) as xfm, \
             nc.sbuf_tensor("ybig", [128, FC, TOK], BF16) as ybig, \
             nc.sbuf_tensor("yh", [128, FC * 4], BF16) as yh, \
             nc.sbuf_tensor("rsqB", [128, TOK], BF16) as rsqB:

            K.dmas("gpsimd", "dmag",
                   [(xfm[:, j, 0:TOK], x_fmT.ap()[128 * j:128 * (j + 1), :])
                    for j in range(FC)] +
                   [(xfm[:, j, TOK:TOK + 4],
                     x_haloT.ap()[128 * j:128 * (j + 1), :])
                    for j in range(FC)])

            if stop == 401:
                _dummy_fin()
                return nc
            rx2 = rx2t
            r_rsY_bf = r_rsY[0:1, 0:TOK]
            r_mukrsk_bf = r_mkrk[0:1, 0:TOK]

            def proj_QK(W, brow_dram, spill_y):
                K.dmas("gpsimd", "dmag",
                       [(WARE[:, k, :], W.ap()[128 * k:128 * (k + 1), :])
                        for k in range(FC)])
                K.dmas("gpsimd", "dmag", [(rowtmp_bf[:, :], brow_dram.ap())])
                if stop == 402:
                    _dummy_fin()
                    return True

                def wsum(t):
                    for k in range(FC):
                        for n in range(2):
                            t.matmul(pBig[0:1, 512 * n:512 * (n + 1)],
                                     onescol[:, :],
                                     WARE[:, k, 512 * n:512 * (n + 1)],
                                     start=(k == 0), stop=(k == FC - 1))
                    return None
                K.task("tensor", wsum)
                if stop == 403:
                    _dummy_fin()
                    return True

                def exb(v):
                    for n in range(2):
                        v.tensor_copy(exS[0:1, 512 * n:512 * (n + 1)],
                                      pBig[0:1, 512 * n:512 * (n + 1)])
                    return None
                K.task("vector", exb)
                if stop == 4035:
                    _dummy_fin()
                    return True
                K.dmas("sync", "dma", [(exS[1:2, :], rowtmp_bf[0:1, :])])
                if stop == 404:
                    _dummy_fin()
                    return True

                for j in range(FC):
                    def main(t, j=j):
                        for k in range(FC):
                            for n in range(4):
                                t.matmul(pBig[:, 512 * n:512 * (n + 1)],
                                         WARE[:, k, 128 * j:128 * (j + 1)],
                                         xfm[:, k, 512 * n:512 * (n + 1)],
                                         start=(k == 0), stop=False)
                            t.matmul(pA[:, 0:4],
                                     WARE[:, k, 128 * j:128 * (j + 1)],
                                     xfm[:, k, TOK:TOK + 4],
                                     start=(k == 0), stop=False)
                        for n in range(4):
                            t.matmul(pBig[:, 512 * n:512 * (n + 1)],
                                     exS[:, 128 * j:128 * (j + 1)],
                                     rx2[:, 512 * n:512 * (n + 1)],
                                     start=False, stop=True)
                        return t.matmul(pA[:, 0:4], exS[:, 128 * j:128 * (j + 1)],
                                        rx2h[:, :], start=False, stop=True)
                    K.task("tensor", main)

                    def evj(s, j=j):
                        s.copy(ybig[:, j, :], pBig[:, 0:TOK])
                        return s.copy(yh[:, 4 * j:4 * (j + 1)], pA[:, 0:4])
                    K.task("scalar", evj)
                    if spill_y:
                        K.dmas("sync", "dma",
                               [(sp_yk.ap()[j, :, :], ybig[:, j, :])])
                    if stop == 405 + j:
                        _dummy_fin()
                        return True

            def stats_of_y(r2, rs_bf, rs32, mu_row, murs_scr, murs_bf, r2h,
                           rsh_bf, tm_out):
                def smm(t):
                    for j in range(FC):
                        for n in range(4):
                            t.matmul(pBig[0:1, 512 * n:512 * (n + 1)],
                                     onescol[:, :],
                                     ybig[:, j, 512 * n:512 * (n + 1)],
                                     start=(j == 0), stop=(j == FC - 1))
                        t.matmul(pA[0:1, 0:4], onescol[:, :],
                                 yh[:, 4 * j:4 * (j + 1)],
                                 start=(j == 0), stop=(j == FC - 1))
                    return None
                K.task("tensor", smm)

                def sev(s):
                    s.copy(mu_row[0:1, 0:TOK], pBig[0:1, 0:TOK])
                    return s.copy(s0row[:, :], pA[0:1, 0:4])
                K.task("scalar", sev)

                for j in range(FC):
                    K.task("scalar", lambda s, j=j: s.activation(
                        stgBF[:, :], ybig[:, j, :], AF.Square))

                    def sqm(t, j=j):
                        for n in range(4):
                            t.matmul(pBig[0:1, 512 * n:512 * (n + 1)],
                                     onescol[:, :],
                                     stgBF[:, 512 * n:512 * (n + 1)],
                                     start=(j == 0), stop=(j == FC - 1))
                        return None
                    K.task("tensor", sqm)
                K.task("scalar", lambda s: s.activation(stgBF[:, 0:FC * 4],
                                                        yh[:, :], AF.Square))

                def sqhm(t):
                    for j in range(FC):
                        t.matmul(pA[0:1, 4:8], onescol[:, :],
                                 stgBF[:, 4 * j:4 * (j + 1)],
                                 start=(j == 0), stop=(j == FC - 1))
                    return None
                K.task("tensor", sqhm)

                def rows1(v):
                    v.tensor_scalar(mu_row[0:1, 0:TOK], mu_row[0:1, 0:TOK],
                                    1.0 / E, None, ALU.mult)
                    v.tensor_scalar(lrow2[0:1, 0:TOK], pBig[0:1, 0:TOK],
                                    1.0 / E, None, ALU.mult)
                    v.tensor_tensor(murs_scr[0:1, 0:TOK], mu_row[0:1, 0:TOK],
                                    mu_row[0:1, 0:TOK], ALU.mult)
                    v.tensor_tensor(lrow2[0:1, 0:TOK], lrow2[0:1, 0:TOK],
                                    murs_scr[0:1, 0:TOK], ALU.subtract)
                    v.tensor_scalar(lrow2[0:1, 0:TOK], lrow2[0:1, 0:TOK],
                                    EPS_LN, None, ALU.add)
                    v.tensor_scalar(s0row[:, 0:4], s0row[:, 0:4], 1.0 / E, None,
                                    ALU.mult)
                    v.tensor_scalar(smallf[0:1, 4:8], pA[0:1, 4:8], 1.0 / E,
                                    None, ALU.mult)
                    v.tensor_tensor(smallf[0:1, 8:12], s0row[0:1, 0:4],
                                    s0row[0:1, 0:4], ALU.mult)
                    v.tensor_tensor(smallf[0:1, 4:8], smallf[0:1, 4:8],
                                    smallf[0:1, 8:12], ALU.subtract)
                    return v.tensor_scalar(smallf[0:1, 4:8], smallf[0:1, 4:8],
                                           EPS_LN, None, ALU.add)
                K.task("vector", rows1)

                def rows_s(s):
                    s.activation(lrow2[0:1, 0:TOK], lrow2[0:1, 0:TOK], AF.Sqrt)
                    return s.activation(smallf[0:1, 4:8], smallf[0:1, 4:8],
                                        AF.Sqrt)
                K.task("scalar", rows_s)

                def rows2(v):
                    v.reciprocal(rs32[0:1, 0:TOK], lrow2[0:1, 0:TOK])
                    v.tensor_copy(rs_bf, rs32[0:1, 0:TOK])
                    if murs_bf is not None:
                        v.tensor_tensor(murs_scr[0:1, 0:TOK], mu_row[0:1, 0:TOK],
                                        rs32[0:1, 0:TOK], ALU.mult)
                        v.tensor_copy(murs_bf, murs_scr[0:1, 0:TOK])
                    v.tensor_scalar(r2[0:1, 0:TOK], mu_row[0:1, 0:TOK], -1.0,
                                    None, ALU.mult)
                    v.tensor_scalar(r2h[0:1, :], s0row[0:1, 0:4], -1.0, None,
                                    ALU.mult)
                    v.tensor_copy(rV[0:1, :], smallf[0:1, 4:8])
                    v.reciprocal(rV2[0:1, :], rV[0:1, :])
                    return v.tensor_copy(smallf[0:1, 8:12], rV2[0:1, :])
                K.task("vector", rows2)
                K.dmas("gpsimd", "dmag", [
                    (r2[1:2, 0:TOK], lrow2[0:1, 0:TOK]),
                    (r2h[1:2, :], smallf[0:1, 4:8]),
                    (rsh_bf[0:1, :], smallf[0:1, 8:12]),
                ])
                if tm_out is not None:
                    K.dmas("sync", "dma",
                           [(tm_out[:, c:c + 1],
                             rs32[0:1, 128 * c:128 * (c + 1)])
                            for c in range(TC)], slow=True)

            if proj_QK(WQ, rows_in["bq_row"], spill_y=False):
                return nc
            if stop == 41:
                _dummy_fin()
                return nc
            stats_of_y(y2, r_rsY_bf, rowD[0:1], lrow, rowC[0:1], None, rq2h,
                       r_rsqh_bf, None)

            _dump("yq0", ybig[:, 0, :], "gpsimd")
            if "qstat" in DBG:
                K.dmas("sync", "dma", [(dbg_t["qstat"].ap()[0:1, :],
                                        lrow[0:1, 0:TOK]),
                                       (dbg_t["qstat"].ap()[1:2, :],
                                        rowD[0:1, 0:TOK])])
            if stop == 42:
                _dummy_fin()
                return nc

            def rsqB_mm(t):
                for n in range(4):
                    t.matmul(pBig[:, 512 * n:512 * (n + 1)], onesrow[0:1, 0:128],
                             r_rsY[0:1, 512 * n:512 * (n + 1)],
                             start=True, stop=True)
                return None
            K.task("tensor", rsqB_mm)
            K.task("scalar", lambda s: s.copy(rsqB[:, :], pBig[:, 0:TOK]))

            def qsum_acc(v):
                for j in range(FC):
                    v.tensor_tensor(junk[:, 0:TOK], ybig[:, j, :], rsqB[:, :],
                                    ALU.mult)
                    v.tensor_reduce(qacc[:, j:j + 1], junk[:, 0:TOK], AX.X,
                                    ALU.add)
                v.tensor_tensor(junk[0:1, 0:TOK], lrow[0:1, 0:TOK],
                                rowD[0:1, 0:TOK], ALU.mult)
                return v.tensor_reduce(s0row[:, 0:1], junk[0:1, 0:TOK], AX.X,
                                       ALU.add)
            K.task("vector", qsum_acc)
            K.dmas("sync", "dma", [(scal1.ap()[0:1], s0row[0:1, 0:1])])
            K.dmas("sync", "dma",
                   [(smallf[:, 15:16],
                     scal1.ap()[0:1].rearrange("(a n) -> a n", a=1)
                     .broadcast_to((128, 1)))])

            def qsum_acc2(v):
                v.tensor_scalar(qacc[:, :], qacc[:, :], smallf[:, 15:16], None,
                                ALU.subtract)
                v.tensor_tensor(qacc[:, :], qacc[:, :], v_fm["gq_fm"][:, :],
                                ALU.mult)
                return v.scalar_tensor_tensor(qacc[:, :], v_fm["Bq_fm"][:, :],
                                              float(TOK), qacc[:, :], ALU.mult,
                                              ALU.add)
            K.task("vector", qsum_acc2)
            _dump("qacc", qacc[:, :])
            K.dmas("gpsimd", "dmag",
                   [(ag3_in.ap().rearrange("(p n) -> p n", p=128), qacc[:, :])])

            def ag3_go(g):
                return g.collective_compute(
                    "AllReduce", ALU.add,
                    replica_groups=[[0, 1, 2, 3], [4, 5, 6, 7]],
                    ins=[ag3_in.ap().opt()], outs=[ag3_out.ap().opt()])
            t_ag3 = K.task("gpsimd", ag3_go)

            if stop == 43:
                _dummy_fin()
                return nc
            K.dmas("gpsimd", "dmag",
                   [(WARE[:, k, :], WPQ.ap()[128 * k:128 * (k + 1), :])
                    for k in range(FC)])

            def wpq_scale(v):
                for k in range(FC):
                    v.tensor_scalar(WARE[:, k, :], WARE[:, k, :],
                                    v_fm["gq_fm"][:, k:k + 1], None, ALU.mult)
                return None
            K.task("vector", wpq_scale)

            def wpq_sum(t):
                for k in range(FC):
                    for n in range(2):
                        t.matmul(pBig[0:1, 512 * n:512 * (n + 1)], onescol[:, :],
                                 WARE[:, k, 512 * n:512 * (n + 1)],
                                 start=(k == 0), stop=(k == FC - 1))
                return None
            K.task("tensor", wpq_sum)
            K.task("scalar", lambda s: s.copy(exS[0:1, :], pBig[0:1, 0:E]))

            def bog_b(v):
                v.reciprocal(smallf[:, 0:FC], v_fm["gq_fm"][:, :])
                v.tensor_tensor(smallf[:, 0:FC], v_fm["Bq_fm"][:, :],
                                smallf[:, 0:FC], ALU.mult)
                return v.tensor_copy(bog[:, :], smallf[:, 0:FC])
            K.task("vector", bog_b)

            def bphi_mm(t):
                for k in range(FC):
                    for n in range(2):
                        t.matmul(pBig[0:1, 1024 + 512 * n:1024 + 512 * (n + 1)],
                                 bog[:, k:k + 1],
                                 WARE[:, k, 512 * n:512 * (n + 1)],
                                 start=(k == 0), stop=(k == FC - 1))
                return None
            K.task("tensor", bphi_mm)
            K.dmas("gpsimd", "dmag",
                   [(rowtmp2[0:1, :], rows_in["bphiq_row"].ap())])

            def bphi_v(v):
                for n in range(2):
                    v.tensor_tensor(lrow[0:1, 512 * n:512 * (n + 1)],
                                    rowtmp2[0:1, 512 * n:512 * (n + 1)],
                                    pBig[0:1, 1024 + 512 * n:
                                         1024 + 512 * (n + 1)], ALU.add)
                return None
            K.task("vector", bphi_v)
            K.dmas("gpsimd", "dmag", [(exS[1:2, :], lrow[0:1, 0:E])])

            if stop == 44:
                _dummy_fin()
                return nc

            def rsqhB_mm(t):
                return t.matmul(pA[:, 4:8], onesrow[0:1, 0:128],
                                r_rsqh_bf[0:1, 0:4], start=True, stop=True)
            K.task("tensor", rsqhB_mm)
            K.task("vector", lambda v: v.tensor_copy(smallf[:, 8:12],
                                                     pA[:, 4:8]))

            for j in range(FC):
                def pq_mm(t, j=j):
                    for k in range(FC):
                        for n in range(4):
                            t.matmul(pBig[:, 512 * n:512 * (n + 1)],
                                     WARE[:, k, 128 * j:128 * (j + 1)],
                                     ybig[:, k, 512 * n:512 * (n + 1)],
                                     start=(k == 0), stop=False)
                        t.matmul(pA[:, 0:4], WARE[:, k, 128 * j:128 * (j + 1)],
                                 yh[:, 4 * k:4 * (k + 1)], start=(k == 0),
                                 stop=False)
                    for n in range(4):
                        t.matmul(pBig[:, 512 * n:512 * (n + 1)],
                                 exS[:, 128 * j:128 * (j + 1)],
                                 y2[:, 512 * n:512 * (n + 1)],
                                 start=False, stop=True)
                    return t.matmul(pA[:, 0:4], exS[:, 128 * j:128 * (j + 1)],
                                    rq2h[:, :], start=False, stop=True)
                K.task("tensor", pq_mm)

                def pq_v(v, j=j):
                    for n in range(4):
                        v.tensor_tensor(junk[:, 512 * n:512 * (n + 1)],
                                        pBig[:, 512 * n:512 * (n + 1)],
                                        rsqB[:, 512 * n:512 * (n + 1)],
                                        ALU.mult)
                    return v.tensor_tensor(smallf[:, 4:8], pA[:, 0:4],
                                           smallf[:, 8:12], ALU.mult)
                K.task("vector", pq_v)

                def pq_tanh(s, j=j):
                    s.activation(stgAF[:, 0:TOK], junk[:, 0:TOK], AF.Tanh)
                    return s.activation(junkb[:, 0:4], smallf[:, 4:8], AF.Tanh)
                K.task("scalar", pq_tanh)
                K.dmas("sync", "dma", [
                    (sp_tvq.ap()[j, :, 2:2 + TOK], stgAF[:, 0:TOK]),
                    (sp_tvq.ap()[j, :, 0:2], junkb[:, 0:2]),
                    (sp_tvq.ap()[j, :, 2 + TOK:4 + TOK], junkb[:, 2:4]),
                ])

            _dump("tvq0", sp_tvq.ap()[0], "gpsimd")
            if stop <= 5:
                _dummy_fin()
                return nc
            proj_QK(WK, rows_in["bk_row"], spill_y=True)
            stats_of_y(y2, r_rsY_bf, rowC[0:1], lrow, rowD[0:1], r_mukrsk_bf, rq2h,
                       r_rsqh_bf, st_rsk_tm)

            _dump("yk0", sp_yk.ap()[0], "gpsimd")
            if "kstat" in DBG:
                K.dmas("sync", "dma", [(dbg_t["kstat"].ap()[0:1, :],
                                        lrow[0:1, 0:TOK]),
                                       (dbg_t["kstat"].ap()[1:2, :],
                                        rowC[0:1, 0:TOK])])
            K.dmas("gpsimd", "dmag",
                   [(WARE[:, k, :], WPK.ap()[128 * k:128 * (k + 1), :])
                    for k in range(FC)])

            def wpk_scale(v):
                for k in range(FC):
                    v.tensor_scalar(WARE[:, k, :], WARE[:, k, :],
                                    v_fm["gk_fm"][:, k:k + 1], None, ALU.mult)
                return None
            K.task("vector", wpk_scale)

            def wpk_sum(t):
                for k in range(FC):
                    for n in range(2):
                        t.matmul(pBig[0:1, 512 * n:512 * (n + 1)], onescol[:, :],
                                 WARE[:, k, 512 * n:512 * (n + 1)],
                                 start=(k == 0), stop=(k == FC - 1))
                return None
            K.task("tensor", wpk_sum)
            K.task("scalar", lambda s: s.copy(exM[0:1, :], pBig[0:1, 0:E]))

            def bogk_b(v):
                v.reciprocal(smallf[:, 0:FC], v_fm["gk_fm"][:, :])
                v.tensor_tensor(smallf[:, 0:FC], v_fm["Bk_fm"][:, :],
                                smallf[:, 0:FC], ALU.mult)
                return v.tensor_copy(bog[:, :], smallf[:, 0:FC])
            K.task("vector", bogk_b)

            def bphik_mm(t):
                for k in range(FC):
                    for n in range(2):
                        t.matmul(pBig[0:1, 1024 + 512 * n:1024 + 512 * (n + 1)],
                                 bog[:, k:k + 1],
                                 WARE[:, k, 512 * n:512 * (n + 1)],
                                 start=(k == 0), stop=(k == FC - 1))
                return None
            K.task("tensor", bphik_mm)
            K.dmas("gpsimd", "dmag",
                   [(rowtmp2[0:1, :], rows_in["bphik_row"].ap())])

            def bphik_v(v):
                for n in range(2):
                    v.tensor_tensor(lrow[0:1, 512 * n:512 * (n + 1)],
                                    rowtmp2[0:1, 512 * n:512 * (n + 1)],
                                    pBig[0:1, 1024 + 512 * n:
                                         1024 + 512 * (n + 1)], ALU.add)
                return None
            K.task("vector", bphik_v)
            K.dmas("gpsimd", "dmag", [(exM[1:2, :], lrow[0:1, 0:E])])

            for c in range(TC):
                def pk_mm(t, c=c):
                    for k in range(FC):
                        for n in range(2):
                            t.matmul(pBig[:, 512 * n:512 * (n + 1)],
                                     ybig[:, k, 128 * c:128 * (c + 1)],
                                     WARE[:, k, 512 * n:512 * (n + 1)],
                                     start=(k == 0), stop=False)
                    for n in range(2):
                        t.matmul(pBig[:, 512 * n:512 * (n + 1)],
                                 y2[:, 128 * c:128 * (c + 1)],
                                 exM[:, 512 * n:512 * (n + 1)],
                                 start=False, stop=True)
                    return None
                K.task("tensor", pk_mm)
                K.task("scalar", lambda s, c=c: s.activation(
                    stgA[:, c % 2, :], pBig[:, 0:E], AF.Tanh,
                    scale=st_rsk_tm[:, c:c + 1]))
                K.dmas("sync", "dma", [(sp_tvk.ap()[c], stgA[:, c % 2, :])])

            if stop <= 6:
                _dummy_fin()
                return nc
            K.dmas("gpsimd", "dmag",
                   [(WARE[:, k, :], WV.ap()[128 * k:128 * (k + 1), :])
                    for k in range(FC)])

            def wv_sum(t):
                for k in range(FC):
                    for n in range(2):
                        t.matmul(pBig[0:1, 512 * n:512 * (n + 1)], onescol[:, :],
                                 WARE[:, k, 512 * n:512 * (n + 1)],
                                 start=(k == 0), stop=(k == FC - 1))
                return None
            K.task("tensor", wv_sum)
            K.dmas("gpsimd", "dmag", [(rowtmp_bf[:, :], rows_in["bv_row"].ap())])

            def wv_exb(v):
                for n in range(2):
                    v.tensor_copy(exM[0:1, 512 * n:512 * (n + 1)],
                                  pBig[0:1, 512 * n:512 * (n + 1)])
                return None
            K.task("vector", wv_exb)
            K.dmas("sync", "dma", [(exM[1:2, :], rowtmp_bf[0:1, :])])

            for c in range(TC):
                def v_mm(t, c=c):
                    for k in range(FC):
                        for n in range(2):
                            t.matmul(pBig[:, 512 * n:512 * (n + 1)],
                                     xfm[:, k, 128 * c:128 * (c + 1)],
                                     WARE[:, k, 512 * n:512 * (n + 1)],
                                     start=(k == 0), stop=False)
                    for n in range(2):
                        t.matmul(pBig[:, 512 * n:512 * (n + 1)],
                                 rx2[:, 128 * c:128 * (c + 1)],
                                 exM[:, 512 * n:512 * (n + 1)],
                                 start=False, stop=True)
                    return None
                K.task("tensor", v_mm)

                def v_st(s, c=c):
                    s.activation(junkb[:, :], pBig[:, 0:E], AF.Identity)
                    return s.activation(stgBF[:, 0:E], pBig[:, 0:E], AF.Square)
                K.task("scalar", v_st)

                def v_rows(v, c=c):
                    v.tensor_reduce(rV[:, 0:1], junkb[:, :], AX.X, ALU.add)
                    v.tensor_scalar(smallf[:, 0:1], rV[:, 0:1], 1.0 / E,
                                    None, ALU.mult)
                    v.tensor_reduce(rV[:, 1:2], stgBF[:, 0:E], AX.X, ALU.add)
                    v.tensor_scalar(rV[:, 1:2], rV[:, 1:2], 1.0 / E, None,
                                    ALU.mult)
                    v.tensor_tensor(rV[:, 2:3], smallf[:, 0:1],
                                    smallf[:, 0:1], ALU.mult)
                    v.tensor_tensor(rV[:, 1:2], rV[:, 1:2], rV[:, 2:3],
                                    ALU.subtract)
                    return v.tensor_scalar(rV[:, 0:1], rV[:, 1:2], EPS_LN,
                                           None, ALU.add)
                K.task("vector", v_rows)
                K.task("scalar", lambda s, c=c: s.activation(
                    rV[:, 0:1], rV[:, 0:1], AF.Sqrt))

                def v_ap(v, c=c):
                    v.reciprocal(rV2[:, 0:1], rV[:, 0:1])
                    v.tensor_tensor(smallf[:, 3:4], smallf[:, 0:1],
                                    rV2[:, 0:1], ALU.mult)
                    v.tensor_scalar(smallf[:, 3:4], smallf[:, 3:4], -1.0, None,
                                    ALU.mult)
                    for n in range(2):
                        v.tensor_scalar(junkb[:, 512 * n:512 * (n + 1)],
                                        pBig[:, 512 * n:512 * (n + 1)],
                                        rV2[:, 0:1], smallf[:, 3:4],
                                        ALU.mult, ALU.add)
                    v.tensor_tensor(junkb[:, :], junkb[:, :], c_gvB[:, :],
                                    ALU.mult)
                    return v.tensor_tensor(stgB[:, c % 2, :], junkb[:, :],
                                           c_bvB[:, :], ALU.add)
                K.task("vector", v_ap)
                K.dmas("sync", "dma", [(sp_vn.ap()[c], stgB[:, c % 2, :])])

        _dump("tvk0", sp_tvk.ap()[0], "gpsimd")
        _dump("vn0", sp_vn.ap()[0], "gpsimd")
        if stop <= 7:
            _dummy_fin()
            return nc
        K.dmas("gpsimd", "dmag",
               [(qg_fm[:, :], ag3_out.ap().rearrange("(p n) -> p n", p=128))],
               waits=[t_ag3])

        def qg_build(v):
            v.tensor_scalar(qg_fm[:, :], qg_fm[:, :], 1.0 / (L * 8.0), None,
                            ALU.mult)
            v.memset(qgp_pad[:, :], 0.0)
            v.memset(qgb_pad[:, :], 0.0)
            v.tensor_tensor(smallf[:, 0:FC], qg_fm[:, :], v_fm["gk_fm"][:, :],
                            ALU.mult)
            for j in range(FC):
                for h2 in range(2):
                    v.tensor_copy(qgp_pad[64 * h2:64 * (h2 + 1),
                                          18 * j + h2:18 * j + h2 + 1],
                                  smallf[64 * h2:64 * (h2 + 1), j:j + 1])
            v.tensor_tensor(smallf[:, 0:FC], qg_fm[:, :], v_fm["Bk_fm"][:, :],
                            ALU.mult)
            for j in range(FC):
                for h2 in range(2):
                    v.tensor_copy(qgb_pad[64 * h2:64 * (h2 + 1),
                                          18 * j + h2:18 * j + h2 + 1],
                                  smallf[64 * h2:64 * (h2 + 1), j:j + 1])
            return None
        K.task("vector", qg_build)
        _dump("qg", qg_fm[:, :])

        def co_mm(t):
            for j in range(FC):
                t.matmul(pA[0:16, 0:1], qgb_pad[:, 16 * j:16 * (j + 1)],
                         onescol[:, :], start=(j == 0), stop=(j == FC - 1))
            for j in range(FC):
                t.matmul(pA[0:16, 1:2], qgp_pad[:, 16 * j:16 * (j + 1)],
                         onescol[:, :], start=(j == 0), stop=(j == FC - 1))
            return None
        K.task("tensor", co_mm)

        def co_ev(s):
            s.copy(co16[:, :], pA[0:16, 0:1])
            return s.copy(qgs16[:, :], pA[0:16, 1:2])
        K.task("scalar", co_ev)
        K.dmas("gpsimd", "dmag",
               [(r_qgs[:, :].rearrange("a (c p) -> a p c", p=16), qgs16[:, :])],
               slow=True)

        ykst = WFLAT.rearrange("p (a n) -> p a n", a=4)
        for j in range(FC):
            K.dmas("sync", "dma", [(ykst[:, j % 2, :], sp_yk.ap()[j])])

            def sc_mm(t, j=j):
                for n in range(4):
                    t.matmul(pBig[0:16, 512 * n:512 * (n + 1)],
                             qgp_pad[:, 16 * j:16 * (j + 1)],
                             ykst[:, j % 2, 512 * n:512 * (n + 1)],
                             start=(j == 0), stop=(j == FC - 1))
                return None
            K.task("tensor", sc_mm)

        def sc_r1(t):
            for n in range(4):
                t.matmul([pA, pB, pC, pD][n][0:16, :], r_qgs[0:1, :],
                         r_mkrk[0:1, 512 * n:512 * (n + 1)], start=True,
                         stop=True)
            return None
        K.task("tensor", sc_r1)
        K.task("scalar", lambda s: s.copy(rowC[0:16, 0:TOK],
                                          pBig[0:16, 0:TOK]))

        def sc_rskB(t):
            for n in range(4):
                t.matmul(pBig[0:16, 512 * n:512 * (n + 1)], ones16[0:1, :],
                         r_rsY[0:1, 512 * n:512 * (n + 1)], start=True,
                         stop=True)
            return None
        K.task("tensor", sc_rskB)

        def sc_fin(v):
            for n in range(4):
                v.tensor_tensor(rowC[0:16, 512 * n:512 * (n + 1)],
                                rowC[0:16, 512 * n:512 * (n + 1)],
                                pBig[0:16, 512 * n:512 * (n + 1)], ALU.mult)
            for n in range(4):
                v.scalar_tensor_tensor(rowC[0:16, 512 * n:512 * (n + 1)],
                                       rowC[0:16, 512 * n:512 * (n + 1)], 1.0,
                                       [pA, pB, pC, pD][n][0:16, :],
                                       ALU.mult, ALU.subtract)
            v.tensor_scalar(rowC[0:16, 0:TOK], rowC[0:16, 0:TOK], co16[:, 0:1],
                            None, ALU.add)
            v.tensor_reduce(sc_Mi[:, :], rowC[0:16, 0:TOK], AX.X, ALU.max)
            v.tensor_tensor(sc_Mi[:, :], sc_Mi[:, :], c_zpad[:, :], ALU.max)
            return v.tensor_scalar(sc_negM[:, :], sc_Mi[:, :], -1.0, None,
                                   ALU.mult)
        K.task("vector", sc_fin)
        K.task("scalar", lambda s: s.activation(
            rowD[0:16, 0:TOK], rowC[0:16, 0:TOK], AF.Exp,
            bias=sc_negM[:, 0:1]))
        K.task("vector", lambda v: v.tensor_reduce(
            sc_Zi[:, :], rowD[0:16, 0:TOK], AX.X, ALU.add))
        K.task("vector", lambda v: v.tensor_tensor(
            sc_t16[:, 0:1], c_zpad[:, :], sc_Mi[:, :], ALU.subtract))
        K.task("scalar", lambda s: s.activation(sc_t16[:, 1:2], sc_t16[:, 0:1],
                                                AF.Exp))
        K.task("vector", lambda v: v.tensor_tensor(sc_Zi[:, :], sc_Zi[:, :],
                                                   sc_t16[:, 1:2], ALU.add))
        if "sc" in DBG:
            K.dmas("sync", "dma", [(dbg_t["sc"].ap(), rowC[0:16, 0:TOK])])
        if "MZ" in DBG:
            K.dmas("sync", "dma", [(dbg_t["MZ"].ap()[:, 0:1], sc_Mi[:, :]),
                                   (dbg_t["MZ"].ap()[:, 1:2], sc_Zi[:, :])],
                   slow=True)
        K.dmas("gpsimd", "dmag", [(ag4_in.ap()[0:16], sc_Mi[:, 0:1]),
                                  (ag4_in.ap()[16:32], sc_Zi[:, 0:1])])

        def ag4_go(g):
            return g.collective_compute(
                "AllGather", ALU.bypass,
                replica_groups=[[0, 1, 2, 3], [4, 5, 6, 7]],
                ins=[ag4_in.ap().opt()], outs=[ag4_out.ap().opt()])
        K.task("gpsimd", ag4_go)
        K.dmas("gpsimd", "dmag",
               [(sc_Mall[:, r:r + 1], ag4_out.ap()[r, 0:16])
                for r in range(4)] +
               [(sc_Zall[:, r:r + 1], ag4_out.ap()[r, 16:32])
                for r in range(4)])

        def comb1(v):
            v.tensor_reduce(sc_Mc[:, :], sc_Mall[:, :], AX.X, ALU.max)
            return v.tensor_scalar(sc_t16[:, 0:4], sc_Mall[:, :],
                                   sc_Mc[:, 0:1], None, ALU.subtract)
        K.task("vector", comb1)
        K.task("scalar", lambda s: s.activation(sc_t16[:, 0:4],
                                                sc_t16[:, 0:4], AF.Exp))

        def comb2(v):
            v.tensor_tensor(sc_t16[:, 0:4], sc_t16[:, 0:4],
                            sc_Zall[:, :], ALU.mult)
            v.tensor_reduce(sc_Zc[:, :], sc_t16[:, 0:4], AX.X, ALU.add)
            v.tensor_tensor(sc_t16[:, 0:1], sc_Mi[:, :], sc_Mc[:, :],
                            ALU.subtract)
            return v.tensor_scalar(sc_t16[:, 1:2], sc_Mc[:, :], -1.0, None,
                                   ALU.mult)
        K.task("vector", comb2)

        def comb3(s):
            s.activation(sc_t16[:, 0:1], sc_t16[:, 0:1], AF.Exp)
            return s.activation(sc_t16[:, 1:2], sc_t16[:, 1:2], AF.Exp)
        K.task("scalar", comb3)

        def alpha_v(v):
            v.reciprocal(sc_fac[:, :], sc_Zc[:, :])
            v.tensor_scalar(sc_fac[:, :], sc_fac[:, :], float(L), None, ALU.mult)
            v.tensor_tensor(sc_a0[:, :], sc_t16[:, 1:2], sc_fac[:, :], ALU.mult)
            v.tensor_tensor(sc_fac[:, :], sc_fac[:, :], sc_t16[:, 0:1], ALU.mult)
            v.tensor_scalar(hm16[:, 0:TOK], rowD[0:16, 0:TOK],
                            sc_fac[:, 0:1], None,
                            ALU.mult)
            return v.tensor_copy(a0_bf[:, :], sc_a0[:, :])
        K.task("vector", alpha_v)
        if "alpha" in DBG:
            K.dmas("gpsimd", "dmag", [(dbg_t["alpha"].ap(),
                                       hm16[0:16, 0:TOK])])

        if stop <= 8:
            _dummy_fin()
            return nc
        with nc.sbuf_tensor("pksb", [128, 2, E], BF16) as pksb, \
             nc.sbuf_tensor("kvacc", [64, 1024], F32) as kvacc, \
             nc.sbuf_tensor("tvkst", [128, 2, E], BF16) as tvkst, \
             nc.sbuf_tensor("vnst", [128, 2, E], BF16) as vnst:
            for c in range(TC):
                K.dmas("sync", "dma", [(tvkst[:, c % 2, :], sp_tvk.ap()[c]),
                                       (vnst[:, c % 2, :], sp_vn.ap()[c])])

                def aE(t, c=c):
                    for n in range(2):
                        t.matmul(pBig[:, 512 * n:512 * (n + 1)],
                                 hm16[:, 128 * c:128 * (c + 1)],
                                 R16[:, 512 * n:512 * (n + 1)],
                                 start=True, stop=True)
                    return None
                K.task("tensor", aE)

                def pks_v(v, c=c):
                    for n in range(2):
                        v.scalar_tensor_tensor(
                            pksb[:, c % 2, 512 * n:512 * (n + 1)],
                            tvkst[:, c % 2, 512 * n:512 * (n + 1)], 1.0,
                            pBig[:, 512 * n:512 * (n + 1)], ALU.add, ALU.mult)
                    return None
                K.task("vector", pks_v)
                if c == 0 and "aE0" in DBG:
                    K.task("scalar", lambda s:
                           (s.copy(junk[:, 0:512], pBig[:, 0:512]),
                            s.copy(junk[:, 512:1024], pBig[:, 512:1024]))[-1])
                    K.dmas("sync", "dma",
                           [(dbg_t["aE0"].ap(), junk[:, 0:1024])])
                if c == 0 and "pks0" in DBG:
                    K.dmas("gpsimd", "dmag",
                           [(dbg_t["pks0"].ap(), pksb[:, 0, :])])

                def kv_mm(t, c=c):
                    for h in range(Hh):
                        t.matmul(pBig[0:64, 1024 + 64 * h:1024 + 64 * (h + 1)],
                                 pksb[:, c % 2, 64 * h:64 * (h + 1)],
                                 vnst[:, c % 2, 64 * h:64 * (h + 1)],
                                 start=True, stop=True)
                    t.matmul(pD[0:1, :], onescol[:, :], pksb[:, c % 2, 0:512],
                             start=(c == 0), stop=(c == TC - 1))
                    t.matmul(pC[0:1, :], onescol[:, :], pksb[:, c % 2, 512:1024],
                             start=(c == 0), stop=(c == TC - 1))
                    return None
                K.task("tensor", kv_mm)

                def kv_acc(v, c=c):
                    if c == 0:
                        v.tensor_copy(kvacc[:, 0:512], pBig[0:64, 1024:1536])
                        v.tensor_copy(kvacc[:, 512:1024],
                                      pBig[0:64, 1536:2048])
                    else:
                        v.tensor_tensor(kvacc[:, 0:512], kvacc[:, 0:512],
                                        pBig[0:64, 1024:1536], ALU.add)
                        v.tensor_tensor(kvacc[:, 512:1024],
                                        kvacc[:, 512:1024],
                                        pBig[0:64, 1536:2048], ALU.add)
                    return None
                K.task("vector", kv_acc)

            def ct_ev(s):
                s.copy(junk[0:64, 0:1024], kvacc[:, :])
                s.copy(lrow[0:1, 0:512], pD[0:1, :])
                return s.copy(lrow[0:1, 512:1024], pC[0:1, :])
            K.task("scalar", ct_ev)
            K.dmas("sync", "dma", [
                (ag5_in.ap()[0:64 * 1024].rearrange("(p n) -> p n", p=64),
                 junk[0:64, 0:1024]),
                (ag5_in.ap()[64 * 1024:65 * 1024].rearrange("(a n) -> a n",
                                                            a=1),
                 lrow[0:1, 0:1024]),
            ])

        def ag5_go(g):
            return g.collective_compute(
                "AllReduce", ALU.add,
                replica_groups=[[0, 1, 2, 3], [4, 5, 6, 7]],
                ins=[ag5_in.ap().opt()], outs=[ag5_out.ap().opt()])
        t_ag5 = K.task("gpsimd", ag5_go)

        if stop <= 9:
            _dummy_fin()
            return nc
        with nc.sbuf_tensor("gu_t", [128, 2, INNER], F32) as gu_t, \
             nc.sbuf_tensor("terng", [128, 1, 1024], BF16) as terng, \
             nc.sbuf_tensor("ternu", [128, 1, 1024], BF16) as ternu, \
             nc.sbuf_tensor("xq4", [128, 2 * FC, 128], BF16) as xq4, \
             nc.sbuf_tensor("guq_tm", [128, INNER], BF16) as guq_tm, \
             nc.sbuf_tensor("guq_fmt", [128, IC, 128], BF16,
                            is_transpose_dest=True) as guq_fmt, \
             nc.sbuf_tensor("ffs", [128, 8], F32) as ffs:
            NP4 = 1 if (90 <= stop < 99) else 8
            for p4 in range(NP4):
                tchunks = [2 * p4, 2 * p4 + 1]
                K.dmas("sync", "dma",
                       [(xq4[:, i * FC + k, :],
                         sp_xq.ap()[k, :, 128 * tc:128 * (tc + 1)])
                        for i, tc in enumerate(tchunks) for k in range(FC)],
                       waits=[t_ag2])

                def rsc(v, tchunks=tchunks):
                    for i, tc in enumerate(tchunks):
                        v.tensor_scalar(ffs[:, i:i + 1], st_r[:, tc:tc + 1],
                                        p_sw[:, 0:1], None, ALU.mult)
                        v.tensor_scalar(ffs[:, 2 + i:3 + i], st_r[:, tc:tc + 1],
                                        p_sw[:, 1:2], None, ALU.mult)
                    return None
                K.task("vector", rsc)
                if stop == 91:
                    _dummy_fin()
                    return nc

                NE4 = 1 if (90 <= stop < 99) else 4
                for e in range(NE4):
                    for k in range(FC):
                        K.dmas("gpsimd", "dmag", [
                            (terng[:, 0, :],
                             ag2v(k, "wg")[:, 1024 * e:1024 * (e + 1)]),
                            (ternu[:, 0, :],
                             ag2v(k, "wu")[:, 1024 * e:1024 * (e + 1)]),
                        ])

                        def gu_mm(t, e=e, k=k):
                            for i in range(2):
                                for n in range(2):
                                    t.matmul(pBig[:, 1024 * i + 512 * n:
                                                  1024 * i + 512 * (n + 1)],
                                             xq4[:, i * FC + k, :],
                                             terng[:, 0,
                                                   512 * n:512 * (n + 1)],
                                             start=(k == 0),
                                             stop=(k == FC - 1))
                                    pp = [pA, pB, pC, pD][2 * i + n]
                                    t.matmul(pp[:, :], xq4[:, i * FC + k, :],
                                             ternu[:, 0,
                                                   512 * n:512 * (n + 1)],
                                             start=(k == 0),
                                             stop=(k == FC - 1))
                            return None
                        K.task("tensor", gu_mm)
                    if stop == 92:
                        _dummy_fin()
                        return nc

                    for i in range(2):
                        K.task("scalar", lambda s, i=i: s.activation(
                            junk[:, 0:1024], pBig[:, 1024 * i:1024 * (i + 1)],
                            AF.Silu, scale=ffs[:, i:i + 1]))

                        def gu_v(v, i=i, e=e):
                            for n in range(2):
                                v.scalar_tensor_tensor(
                                    gu_t[:, i, 1024 * e + 512 * n:
                                         1024 * e + 512 * (n + 1)],
                                    [pA, pB, pC, pD][2 * i + n][:, :],
                                    ffs[:, 2 + i:3 + i],
                                    junk[:, 512 * n:512 * (n + 1)],
                                    ALU.mult, ALU.mult)
                            return None
                        K.task("vector", gu_v)
                    if stop == 93:
                        _dummy_fin()
                        return nc

                for i, tc in enumerate(tchunks):
                    def quant(v, i=i):
                        v.tensor_reduce(rV[:, 0:1], gu_t[:, i, :], AX.X,
                                        ALU.max, apply_absolute_value=True)
                        v.tensor_scalar(rV[:, 1:2], rV[:, 0:1], 1e-5,
                                        1.0 / 127.0, ALU.max, ALU.mult)
                        v.tensor_copy(rV[:, 0:1], rV[:, 1:2])
                        v.reciprocal(rV2[:, 0:1], rV[:, 0:1])
                        v.tensor_scalar(rV[:, 1:2], rV[:, 1:2],
                                        p_sw[:, 2:3], None, ALU.mult)
                        v.tensor_scalar(gu_t[:, i, :], gu_t[:, i, :],
                                        rV2[:, 0:1], CROUND, ALU.mult, ALU.add)
                        return v.tensor_scalar(guq_tm[:, :], gu_t[:, i, :],
                                               -CROUND, rV[:, 1:2], ALU.add,
                                               ALU.mult)
                    K.task("vector", quant)
                    if stop == 94:
                        _dummy_fin()
                        return nc
                    K.task("scalar", lambda s:
                           (s.dma_start_transpose(guq_fmt[:, :, :],
                                                  guq_tm[:, :])
                            .then_inc(K.sem["dmas"], 16), 1), dma="dmas")
                    if stop == 95:
                        _dummy_fin()
                        return nc
                    K.dmas("sync", "dma", [(sp_guq.ap()[tc], guq_fmt[:, :, :])])
                    if stop == 96:
                        _dummy_fin()
                        return nc

        _dump("guq00", sp_guq.ap()[0][:, 0, :], "gpsimd")
        _dump("psw", p_sw[:, :])
        if stop <= 10:
            _dummy_fin()
            return nc
        with nc.sbuf_tensor("attnA", [128, FC, HTOK], BF16) as attnA:
            with nc.sbuf_tensor("tvqr", [128, FC, HTOK], BF16) as tvqr, \
                 nc.sbuf_tensor("pqr", [128, 1032], BF16) as pqr, \
                 nc.sbuf_tensor("ctac", [65, 1024], F32) as ctac:

                K.dmas("gpsimd", "dmag",
                       [(ctac[:, :],
                         ag5_out.ap().rearrange("(p n) -> p n", p=65))],
                       waits=[t_ag5])

                _dump("ct", ctac[:, :])
                K.task("scalar", lambda s: s.activation(
                    pk0_fm[:, :], v_fm["bphik_fm"][:, :], AF.Tanh))

                def a0_mm(t):
                    for j in range(FC):
                        t.matmul(pA[:, j:j + 1], R16[:, 128 * j:128 * (j + 1)],
                                 a0_bf[:, :], start=True, stop=True)
                    return None
                K.task("tensor", a0_mm)
                K.dmas("sync", "dma",
                       [(ksum_fm[:, c:c + 1],
                         ctac[64:65, 128 * c:128 * (c + 1)])
                        for c in range(FC)], slow=True)

                def ks_build(v):
                    v.tensor_scalar(pk0_fm[:, :], pk0_fm[:, :], 1.0, None,
                                    ALU.add)
                    v.tensor_tensor(smallf[:, 0:FC], pk0_fm[:, :], pA[:, 0:FC],
                                    ALU.mult)
                    v.tensor_tensor(ksum_fm[:, :], ksum_fm[:, :],
                                    smallf[:, 0:FC], ALU.add)
                    v.memset(ks_pad[:, :], 0.0)
                    for j in range(FC):
                        for h2 in range(2):
                            v.tensor_copy(ks_pad[64 * h2:64 * (h2 + 1),
                                                 18 * j + h2:18 * j + h2 + 1],
                                          ksum_fm[64 * h2:64 * (h2 + 1),
                                                  j:j + 1])
                    v.memset(kvbd[:, :, :], 0.0)
                    for j in range(FC):
                        v.tensor_copy(kvbd[0:64, j, 0:64],
                                      ctac[0:64, 128 * j:128 * j + 64])
                    return None
                K.task("vector", ks_build)
                _dump("ksum", ksum_fm[:, :])
                K.dmas("gpsimd", "dmag",
                       [(kvbd[64:128, j, 64:128],
                         ctac[0:64, 128 * j + 64:128 * (j + 1)])
                        for j in range(FC)])

                def hs_mm(t):
                    for j in range(FC):
                        t.matmul(pA[0:16, 8:9], ks_pad[:, 16 * j:16 * (j + 1)],
                                 onescol[:, :], start=(j == 0),
                                 stop=(j == FC - 1))
                    return None
                K.task("tensor", hs_mm)
                K.task("scalar", lambda s: s.copy(hs16[:, :], pA[0:16, 8:9]))
                K.dmas("gpsimd", "dmag",
                       [(r_hsum[:, :].rearrange("a (c p) -> a p c", p=16),
                         hs16[:, :])], slow=True)
                K.dmas("sync", "dma", [(tvqr[:, j, :], sp_tvq.ap()[j])
                                       for j in range(FC)])

                NSL = [(0, 512), (512, 512), (1024, 512), (1536, 512)]

                def den_mm(t):
                    for j in range(FC):
                        for off, ln in NSL:
                            t.matmul(pBig[0:16, off:off + ln],
                                     ks_pad[:, 16 * j:16 * (j + 1)],
                                     tvqr[:, j, off:off + ln],
                                     start=(j == 0), stop=False)
                        t.matmul(pA[0:16, 12:16],
                                 ks_pad[:, 16 * j:16 * (j + 1)],
                                 tvqr[:, j, 2048:2052],
                                 start=(j == 0), stop=False)
                    for off, ln in NSL:
                        t.matmul(pBig[0:16, off:off + ln], r_hsum[0:1, :],
                                 onesrow[0:1, off:off + ln], start=False,
                                 stop=True)
                    return t.matmul(pA[0:16, 12:16], r_hsum[0:1, :],
                                    onesrow[0:1, 2048:2052], start=False,
                                    stop=True)
                K.task("tensor", den_mm)

                def den_v(v):
                    for n in range(4):
                        v.tensor_scalar(rowC[0:16, 512 * n:512 * (n + 1)],
                                        pBig[0:16, 512 * n:512 * (n + 1)],
                                        EPS_DEN, None, ALU.add)
                    v.tensor_scalar(rowC[0:16, TOK:HTOK], pA[0:16, 12:16],
                                    EPS_DEN, None, ALU.add)
                    v.reciprocal(rowC[0:16, 0:HTOK], rowC[0:16, 0:HTOK])
                    return v.tensor_copy(hm16[:, 0:HTOK], rowC[0:16, 0:HTOK])
                K.task("vector", den_v)
                if "rden" in DBG:
                    K.dmas("sync", "dma", [(dbg_t["rden"].ap(),
                                            rowC[0:16, 0:HTOK])])

                for j in range(FC):
                    for hh in range(2):
                        off = hh * 1026

                        def rdB(t, j=j, off=off):
                            for n0 in range(0, 1026, 512):
                                nn = min(512, 1026 - n0)
                                t.matmul(pBig[:, n0:n0 + nn],
                                         R16[:, 128 * j:128 * (j + 1)],
                                         hm16[:, off + n0:off + n0 + nn],
                                         start=True, stop=True)
                            return None
                        K.task("tensor", rdB)

                        def pqr_v(v, j=j, off=off):
                            for n0, nn in [(0, 512), (512, 512), (1024, 2)]:
                                v.scalar_tensor_tensor(
                                    pqr[:, n0:n0 + nn],
                                    tvqr[:, j, off + n0:off + n0 + nn], 1.0,
                                    pBig[:, n0:n0 + nn], ALU.add, ALU.mult)
                            return None
                        K.task("vector", pqr_v)

                        def num_mm(t, j=j):
                            for n0, pp in [(0, pA), (512, pB)]:
                                t.matmul(pp[:, 0:512], kvbd[:, j, :],
                                         pqr[:, n0:n0 + 512], start=True,
                                         stop=True)
                            return t.matmul(pC[:, 0:2], kvbd[:, j, :],
                                            pqr[:, 1024:1026], start=True,
                                            stop=True)
                        K.task("tensor", num_mm)

                        def attn_ev(s, j=j, off=off):
                            s.copy(attnA[:, j, off:off + 512], pA[:, 0:512])
                            s.copy(attnA[:, j, off + 512:off + 1024],
                                   pB[:, 0:512])
                            return s.copy(attnA[:, j, off + 1024:off + 1026],
                                          pC[:, 0:2])
                        K.task("scalar", attn_ev)

            if stop <= 11:
                _dummy_fin()
                return nc
            with nc.sbuf_tensor("attnB", [128, FC, TOK], BF16) as attnB:
                ABF = attnB.ap().rearrange("p a n -> p (a n)")

                def mask_v(v):
                    for j in range(FC):
                        v.tensor_tensor(attnA[:, j, 0:2], attnA[:, j, 0:2],
                                        c_cmask[:, 0:2], ALU.mult)
                        v.tensor_tensor(attnA[:, j, TOK + 2:TOK + 4],
                                        attnA[:, j, TOK + 2:TOK + 4],
                                        c_cmask[:, 2:4], ALU.mult)
                    return None
                K.task("vector", mask_v)
                _dump("attn0", attnA[:, 0, :], "gpsimd")

                K.dmas("gpsimd", "dmag",
                       [(WFLAT[:, 128 * (j * 5 + k5):128 * (j * 5 + k5 + 1)],
                         convw_bd.ap()[j, k5])
                        for j in range(FC) for k5 in range(5)])
                K.task("vector", lambda v: v.tensor_scalar(
                    gcb_fm[:, :], v_fm["convb_fm"][:, :], c_gammaB[:, 0:1],
                    None, ALU.mult))

                for j in range(FC):
                    for half in range(2):
                        o0 = half * 1024

                        def conv_mm(t, j=j, o0=o0):
                            for k5 in range(5):
                                idx = j * 5 + k5
                                for n in range(2):
                                    t.matmul(
                                        pBig[:, 512 * n:512 * (n + 1)],
                                        WFLAT[:, 128 * idx:128 * (idx + 1)],
                                        attnA[:, j, k5 + o0 + 512 * n:
                                              k5 + o0 + 512 * (n + 1)],
                                        start=(k5 == 0), stop=(k5 == 4))
                            return None
                        K.task("tensor", conv_mm)
                        K.task("scalar", lambda s, j=j: s.activation(
                            stgC[:, 0, :], pBig[:, 0:1024], AF.Identity,
                            bias=gcb_fm[:, j:j + 1], scale=c_gammaB[:, 0:1]))
                        K.task("vector", lambda v, j=j, o0=o0: v.tensor_tensor(
                            attnB[:, j, o0:o0 + 1024], stgC[:, 0, :],
                            attnA[:, j, 2 + o0:2 + o0 + 1024], ALU.add))

                _dump("attn20", attnB[:, 0, :], "gpsimd")
                for j in range(FC):
                    K.task("scalar", lambda s, j=j: s.activation(
                        attnA[:, j, 0:TOK], attnB[:, j, :], AF.Square))

                def lna_mm(t):
                    for j in range(FC):
                        for n in range(4):
                            t.matmul(pBig[0:1, 512 * n:512 * (n + 1)],
                                     onescol[:, :],
                                     attnB[:, j, 512 * n:512 * (n + 1)],
                                     start=(j == 0), stop=(j == FC - 1))
                    for n in range(4):
                        pp = [pA, pB, pC, pD][n]
                        for j in range(FC):
                            t.matmul(pp[0:1, :], onescol[:, :],
                                     attnA[:, j, 512 * n:512 * (n + 1)],
                                     start=(j == 0), stop=(j == FC - 1))
                    return None
                K.task("tensor", lna_mm)

                def lna_rows(v):
                    for n in range(4):
                        v.tensor_scalar(rowC[0:1, 512 * n:512 * (n + 1)],
                                        pBig[0:1, 512 * n:512 * (n + 1)],
                                        1.0 / E, None, ALU.mult)
                    for n in range(4):
                        v.tensor_scalar(lrow2[0:1, 512 * n:512 * (n + 1)],
                                        [pA, pB, pC, pD][n][0:1, :], 1.0 / E,
                                        None, ALU.mult)
                    v.tensor_tensor(lrow[0:1, 0:TOK], rowC[0:1, 0:TOK],
                                    rowC[0:1, 0:TOK], ALU.mult)
                    v.tensor_tensor(lrow2[0:1, 0:TOK], lrow2[0:1, 0:TOK],
                                    lrow[0:1, 0:TOK], ALU.subtract)
                    return v.tensor_scalar(lrow2[0:1, 0:TOK], lrow2[0:1, 0:TOK],
                                           EPS_LN, None, ALU.add)
                K.task("vector", lna_rows)
                K.task("scalar", lambda s: s.activation(
                    lrow2[0:1, 0:TOK], lrow2[0:1, 0:TOK], AF.Sqrt))

                def lna_rows2(v):
                    v.reciprocal(lrow[0:1, 0:TOK], lrow2[0:1, 0:TOK])
                    v.tensor_copy(r_rsY[0:1, 0:TOK], lrow[0:1, 0:TOK])
                    v.tensor_tensor(lrow[0:1, 0:TOK], rowC[0:1, 0:TOK],
                                    lrow[0:1, 0:TOK], ALU.mult)
                    v.tensor_scalar(lrow[0:1, 0:TOK], lrow[0:1, 0:TOK], -1.0,
                                    None, ALU.mult)
                    return v.tensor_copy(r_mkrk[0:1, 0:TOK], lrow[0:1, 0:TOK])
                K.task("vector", lna_rows2)

                for half in range(2):
                    o0 = half * 1024

                    def lap_b(t, o0=o0):
                        for n in range(2):
                            t.matmul(pBig[:, 512 * n:512 * (n + 1)],
                                     onesrow[0:1, 0:128],
                                     r_rsY[0:1,
                                           o0 + 512 * n:o0 + 512 * (n + 1)],
                                     start=True, stop=True)
                            t.matmul(pBig[:, 1024 + 512 * n:
                                          1024 + 512 * (n + 1)],
                                     onesrow[0:1, 0:128],
                                     r_mkrk[0:1,
                                            o0 + 512 * n:o0 + 512 * (n + 1)],
                                     start=True, stop=True)
                        return None
                    K.task("tensor", lap_b)

                    for j in range(FC):
                        def lap_v(v, j=j, o0=o0):
                            for n in range(2):
                                v.tensor_tensor(
                                    junk[:, 512 * n:512 * (n + 1)],
                                    attnB[:, j, o0 + 512 * n:
                                          o0 + 512 * (n + 1)],
                                    pBig[:, 512 * n:512 * (n + 1)], ALU.mult)
                                v.tensor_tensor(
                                    attnA[:, j, o0 + 512 * n:
                                          o0 + 512 * (n + 1)],
                                    junk[:, 512 * n:512 * (n + 1)],
                                    pBig[:, 1024 + 512 * n:
                                         1024 + 512 * (n + 1)], ALU.add)
                            return None
                        K.task("vector", lap_v)

                _dump("ln0", attnA[:, 0, 0:TOK], "gpsimd")

                def boga_b(v):
                    v.reciprocal(smallf[:, 0:FC], v_fm["ga_fm"][:, :])
                    v.tensor_tensor(smallf[:, 0:FC], v_fm["Ba_fm"][:, :],
                                    smallf[:, 0:FC], ALU.mult)
                    return v.tensor_copy(bog[:, :], smallf[:, 0:FC])
                K.task("vector", boga_b)

                w1st = WFLAT.rearrange("p (a n) -> p a n", a=2)
                for half in range(2):
                    for k in range(FC):
                        K.dmas("gpsimd", "dmag",
                               [(w1st[:, k % 2, :], ag2v(k, "w1"))],
                               waits=[t_ag2])

                        def b1_mm(t, k=k, half=half):
                            for n in range(4):
                                t.matmul(pBig[0:1, 512 * n:512 * (n + 1)],
                                         bog[:, k:k + 1],
                                         w1st[:, k % 2,
                                              half * 2048 + 512 * n:
                                              half * 2048 + 512 * (n + 1)],
                                         start=(k == 0), stop=(k == FC - 1))
                            return None
                        K.task("tensor", b1_mm)
                    K.task("scalar", lambda s: s.copy(lrow[0:1, 0:TOK],
                                                      pBig[0:1, 0:TOK]))
                    K.dmas("sync", "dma",
                           [(b1r_fm[:, 16 * half + c:16 * half + c + 1],
                             lrow[0:1, 128 * c:128 * (c + 1)])
                            for c in range(16)], slow=True)

                K.dmas("gpsimd", "dmag",
                       [(rowtmp2[0:1, :], rows_in["b2_row"].ap())])
                K.dmas("sync", "dma",
                       [(lrow[0:1, 0:E].rearrange("a n -> a n"),
                         rows_in["Ba_row"].ap())])

                def b1p_v(v):
                    v.tensor_tensor(v_b1p[:, :], v_b1[:, :], b1r_fm[:, :],
                                    ALU.add)
                    v.tensor_tensor(rowtmp2[0:1, :], rowtmp2[0:1, :],
                                    lrow[0:1, 0:E], ALU.add)
                    v.tensor_copy(b2pBa_bf[:, :], rowtmp2[0:1, :])
                    v.tensor_copy(junkb[:, :], c_gaB32[:, :])
                    for j in range(FC):
                        v.tensor_tensor(c_gaI[:, 128 * j:128 * (j + 1)],
                                        ident[:, :],
                                        junkb[:, 128 * j:128 * (j + 1)],
                                        ALU.mult)
                    return None
                K.task("vector", b1p_v)

                if stop <= 12:
                    _dummy_fin()
                    return nc
                for q4 in range(4):
                    t0 = 512 * q4
                    tchunks = [q4 * 4 + i for i in range(4)]
                    for o in range(IC):
                        K.dmas("gpsimd", "dmag",
                               [(stgA[:, o % 2, 128 * k:128 * (k + 1)],
                                 ag2v(k, "w1")[:, 128 * o:128 * (o + 1)])
                                for k in range(FC)])

                        def w1_mm(t, o=o, t0=t0):
                            pp = [pA, pB][o % 2]
                            for k in range(FC):
                                t.matmul(pp[:, :],
                                         stgA[:, o % 2, 128 * k:128 * (k + 1)],
                                         attnA[:, k, t0:t0 + 512],
                                         start=(k == 0), stop=(k == FC - 1))
                            return None
                        K.task("tensor", w1_mm)
                        K.task("scalar", lambda s, o=o: s.activation(
                            ABF[:, 512 * o:512 * (o + 1)],
                            [pA, pB][o % 2][:, :], AF.Gelu,
                            bias=v_b1p[:, o:o + 1]))

                    for i, tc in enumerate(tchunks):
                        K.dmas("sync", "dma",
                               [(stgC[:, 0, :],
                                 sp_outacc.ap()[128 * tc:128 * (tc + 1), :])])
                        for o in range(IC):
                            r, roff = o // 4, (o % 4) * 128
                            K.dmas("gpsimd", "dmag", [
                                (stgB[:, o % 2, 0:E],
                                 ag2v(r, "w2")[roff:roff + 128, :]),
                                (WST[:, o % 2, 0:E],
                                 ag2v(r, "wd")[roff:roff + 128, :]),
                            ])
                            K.dmas("sync", "dma",
                                   [(WST[:, o % 2, E:E + 128],
                                     sp_guq.ap()[tc][:, o, :])])

                            def fin_mm(t, o=o, i=i):
                                for n in range(2):
                                    t.matmul(pBig[:, 512 * n:512 * (n + 1)],
                                             ABF[:, 512 * o + 128 * i:
                                                 512 * o + 128 * (i + 1)],
                                             stgB[:, o % 2,
                                                  512 * n:512 * (n + 1)],
                                             start=(o == 0), stop=False)
                                    t.matmul(pBig[:, 512 * n:512 * (n + 1)],
                                             WST[:, o % 2, E:E + 128],
                                             WST[:, o % 2,
                                                 512 * n:512 * (n + 1)],
                                             start=False, stop=False)
                                return None
                            K.task("tensor", fin_mm)

                        def fin_tail(t, tc=tc):
                            for k in range(FC):
                                t.matmul(pBig[:, 128 * k:128 * (k + 1)],
                                         attnA[:, k, 128 * tc:128 * (tc + 1)],
                                         c_gaI[:, 128 * k:128 * (k + 1)],
                                         start=False, stop=False)
                            for n in range(2):
                                t.matmul(pBig[:, 512 * n:512 * (n + 1)],
                                         onesrow[0:1, 0:128],
                                         b2pBa_bf[0:1, 512 * n:512 * (n + 1)],
                                         start=False, stop=True)
                            return None
                        K.task("tensor", fin_tail)
                        def fin_v(v, tc=tc):
                            for n in range(2):
                                v.tensor_tensor(
                                    fout[:, 512 * n:512 * (n + 1)],
                                    pBig[:, 512 * n:512 * (n + 1)],
                                    stgC[:, 0, 512 * n:512 * (n + 1)], ALU.add)
                            return None
                        K.task("vector", fin_v)
                        K.dmas("sync", "dma",
                               [(out.ap()[128 * tc:128 * (tc + 1), :],
                                 fout[:, :])])

        K.task("sync", lambda q: None)

    return nc


def _host_constants():
    import ml_dtypes
    bf = ml_dtypes.bfloat16
    ident = np.eye(128, dtype=bf)
    R16 = np.zeros((16, E), dtype=bf)
    for h in range(Hh):
        R16[h, h * Dd:(h + 1) * Dd] = 1
    onesrow = np.ones((1, HTOK), dtype=bf)
    onescol = np.ones((128, 1), dtype=bf)
    ones16 = np.ones((1, 16), dtype=bf)
    return ident, R16, onesrow, onescol, ones16


def make_in_maps(inputs):
    x = np.asarray(inputs["L0_attn_x"], dtype=np.float32)
    g = lambda n: np.ascontiguousarray(np.asarray(inputs[n], dtype=np.float32))
    Wq, Wk, Wv = g("L0_attn_Wq"), g("L0_attn_Wk"), g("L0_attn_Wv")
    Wpq, Wpk = g("L0_attn_Wphi_q"), g("L0_attn_Wphi_k")
    W1, W2 = g("L0_attn_W1"), g("L0_attn_W2")
    wgT = np.ascontiguousarray(g("L0_ffn_w_gate").T)
    wuT = np.ascontiguousarray(g("L0_ffn_w_up").T)
    wdT = np.ascontiguousarray(g("L0_ffn_w_down").T)
    conv_w = g("L0_attn_conv_w")
    gamma = np.float32(np.asarray(inputs["L0_attn_gamma"], dtype=np.float32))
    vec = {k: g(n) for k, n in [
        ("bq", "L0_attn_bq"), ("bk", "L0_attn_bk"), ("bv", "L0_attn_bv"),
        ("gq", "L0_attn_lnq_g"), ("Bq", "L0_attn_lnq_b"),
        ("gk", "L0_attn_lnk_g"), ("Bk", "L0_attn_lnk_b"),
        ("gv", "L0_attn_lnv_g"), ("Bv", "L0_attn_lnv_b"),
        ("ga", "L0_attn_lna_g"), ("Ba", "L0_attn_lna_b"),
        ("bphiq", "L0_attn_bphi_q"), ("bphik", "L0_attn_bphi_k"),
        ("b1", "L0_attn_b1"), ("b2", "L0_attn_b2"), ("cb", "L0_attn_conv_b")]}
    fm = lambda v: np.ascontiguousarray(v.reshape(FC, 128).T)
    fm32 = lambda v: np.ascontiguousarray(v.reshape(IC, 128).T)
    row = lambda v: np.ascontiguousarray(v.reshape(1, -1).astype(np.float32))
    convbd = np.zeros((FC, 5, 128, 128), np.float32)
    for j in range(FC):
        for k5 in range(5):
            for half in range(2):
                ob = 128 * j + 64 * half
                blkw = conv_w[ob:ob + 64, :, k5]
                convbd[j, k5, 64 * half:64 * half + 64,
                       64 * half:64 * half + 64] = blkw.T
    ident, R16c, onesrow, onescol, ones16 = _host_constants()
    in_maps = []
    for c in range(NCORE):
        b = c // 4
        t0 = TOK * (c % 4)
        xs = np.ascontiguousarray(x[b, t0:t0 + TOK])
        halo = np.zeros((4, E), np.float32)
        if t0 - 2 >= 0:
            halo[0:2] = x[b, t0 - 2:t0]
        if t0 + TOK + 2 <= L:
            halo[2:4] = x[b, t0 + TOK:t0 + TOK + 2]
        cmask = np.ones((128, 4), np.float32)
        if t0 == 0:
            cmask[:, 0:2] = 0
        if t0 + TOK == L:
            cmask[:, 2:4] = 0
        zpad = np.full((16, 1), 0.0 if t0 == 0 else -1e30, np.float32)
        m = {
            "x_tm": xs, "x_fmT": np.ascontiguousarray(xs.T),
            "x_halo": halo, "x_haloT": np.ascontiguousarray(halo.T),
            "Wq": Wq, "Wk": Wk, "Wv": Wv, "Wphi_q": Wpq, "Wphi_k": Wpk,
            "w2_slice": np.ascontiguousarray(W2[512 * c:512 * (c + 1)]),
            "wg_slice": np.ascontiguousarray(wgT[128 * c:128 * (c + 1)]),
            "wu_slice": np.ascontiguousarray(wuT[128 * c:128 * (c + 1)]),
            "wd_slice": np.ascontiguousarray(wdT[512 * c:512 * (c + 1)]),
            "w1_slice": np.ascontiguousarray(W1[128 * c:128 * (c + 1)]),
            "ga_slice": np.ascontiguousarray(
                vec["ga"][128 * c:128 * (c + 1)].reshape(128, 1)),
            "convw_bd": convbd,
            "bq_fm": fm(vec["bq"]), "bk_fm": fm(vec["bk"]),
            "gq_fm": fm(vec["gq"]), "Bq_fm": fm(vec["Bq"]),
            "gk_fm": fm(vec["gk"]), "Bk_fm": fm(vec["Bk"]),
            "ga_fm": fm(vec["ga"]), "Ba_fm": fm(vec["Ba"]),
            "bphik_fm": fm(vec["bphik"]), "convb_fm": fm(vec["cb"]),
            "b1_fm": fm32(vec["b1"]),
            "bq_row": row(vec["bq"]), "bk_row": row(vec["bk"]),
            "bv_row": row(vec["bv"]), "bphiq_row": row(vec["bphiq"]),
            "bphik_row": row(vec["bphik"]), "b2_row": row(vec["b2"]),
            "Ba_row": row(vec["Ba"]),
            "gvB": np.ascontiguousarray(np.broadcast_to(vec["gv"], (128, E))),
            "bvB": np.ascontiguousarray(np.broadcast_to(vec["Bv"], (128, E))),
            "gaB": np.ascontiguousarray(np.broadcast_to(vec["ga"], (128, E))),
            "gammaB": np.full((128, 1), gamma, np.float32),
            "cmask": cmask, "zpad": zpad,
            "ident_in": ident, "R16_in": R16c, "onesrow_in": onesrow,
            "onescol_in": onescol, "ones16_in": ones16,
        }
        in_maps.append(m)
    return in_maps


def kernel(**inputs):
    in_maps = make_in_maps(inputs)
    nc = bass.Bass("TRN2", target_bir_lowering=False)
    build(nc)
    res = run_bass_kernel_spmd(nc, in_maps, core_ids=list(range(NCORE)))
    outf = np.zeros((B, L, E), np.float32)
    for c in range(NCORE):
        b = c // 4
        t0 = TOK * (c % 4)
        outf[b, t0:t0 + TOK] = res.results[c]["out"]
    return outf


# version 11
# speedup vs baseline: 1.0406x; 1.0406x over previous
import numpy as np
from contextlib import ExitStack

import concourse.bass as bass
import concourse.mybir as mybir
from concourse.bass_utils import run_bass_kernel_spmd

F32 = mybir.dt.float32
BF16 = mybir.dt.bfloat16
AF = mybir.ActivationFunctionType
ALU = mybir.AluOpType
AX = mybir.AxisListType

B, L, E, Hh, Dd = 2, 8192, 1024, 16, 64
INNER = 4096
TOK = 2048
TC = TOK // 128
FC = E // 128
IC = INNER // 128
NCORE = 8
CROUND = float(2**23 + 2**22)
EPS_LN = 1e-5
EPS_DEN = 1e-6
HTOK = TOK + 4

AG2_WG = 128 * INNER
AG2_WU = 128 * INNER
AG2_WD = 512 * E
AG2_W2 = 512 * E
AG2_W1 = 128 * INNER
AG2_PER = AG2_WG + AG2_WU + AG2_WD + AG2_W2 + AG2_W1
AG2_OFF = {"wg": 0, "wu": AG2_WG, "wd": AG2_WG + AG2_WU,
           "w2": AG2_WG + AG2_WU + AG2_WD, "w1": AG2_WG + AG2_WU + AG2_WD + AG2_W2}


class Bld:
    def __init__(self, nc, es):
        self.nc = nc
        self.blk = es.enter_context(nc.Block())
        self.sem = {}
        self.cnt = {}
        for e in ["tensor", "vector", "scalar", "gpsimd", "sync",
                  "dma", "dmas", "dmag", "fence"]:
            self.sem[e] = es.enter_context(nc.semaphore("sem_" + e))
            self.cnt[e] = 0
        self.chain = None
        self.tokbuf = None

    FENCE_OPS = {"tensor_reduce", "reduce_max", "reduce_sum", "reciprocal"}

    class _Rec:
        def __init__(self, eng, bld):
            self._eng = eng
            self._bld = bld
            self._last = None
            self._fenced = None
            self._fence_tok = None
            self._dirty = set()

        @staticmethod
        def _apname(o):
            t = getattr(o, "tensor", None)
            return getattr(t, "name", None) if t is not None else None

        def _fence_now(self, ins):
            b = self._bld
            b.cnt["fence"] += 1
            ins.then_inc(b.sem["fence"], 1)
            self._eng.wait_ge(b.sem["fence"], b.cnt["fence"])
            self._fenced = ins
            self._fence_tok = ("fence", b.cnt["fence"])
            self._last = None
            self._dirty.clear()

        def __getattr__(self, n):
            a = getattr(self._eng, n)
            if callable(a):
                if n.startswith("dma") or n in ("wait_ge", "nop",
                                                "collective_compute"):
                    return a

                def w(*args, **kw):
                    ins_n = set()
                    outs_n = set()
                    for i, o in enumerate(args):
                        nm = Bld._Rec._apname(o)
                        if nm:
                            (outs_n if i == 0 else ins_n).add(nm)
                    for k, o in kw.items():
                        nm = Bld._Rec._apname(o)
                        if nm:
                            (outs_n if k in ("out", "accum_out")
                             else ins_n).add(nm)
                    if (self._dirty & ins_n) and self._last is not None:
                        self._fence_now(self._last)
                    r = a(*args, **kw)
                    if hasattr(r, "then_inc"):
                        self._last = r
                        self._dirty |= outs_n
                        if n in Bld.FENCE_OPS or "accum_out" in kw:
                            self._fence_now(r)
                    return r
                return w
            return a

    def task(self, engine, emit, waits=(), dma=None, chain=True):
        sem_name = dma if dma else engine
        waits = list(waits)
        if chain and self.chain is not None:
            waits.append(self.chain)
        holder = {}

        def f(eng):
            rec = Bld._Rec(eng, self)
            for (s, v) in waits:
                if v > 0:
                    eng.wait_ge(self.sem[s], v)
            r = emit(rec)
            if dma:
                ins, n = r
                self.cnt[sem_name] += 16 * n
            else:
                ins = r if (r is not None and hasattr(r, "then_inc"))                     else rec._last
                if ins is not None and ins is getattr(rec, "_fenced", None):
                    ins = None
                if ins is None and rec._fence_tok is not None:
                    holder["tok"] = rec._fence_tok
                    return
                if ins is None:
                    ins = eng.nop()
                self.cnt[sem_name] += 1
                ins.then_inc(self.sem[sem_name], 1)
            holder["tok"] = (sem_name, self.cnt[sem_name])

        getattr(self.blk, engine)(f)
        tok = holder["tok"]
        if chain:
            self.chain = tok
        return tok

    def dmas(self, engine, sem_name, pairs, waits=(), chain=True, slow=False):
        def emit(eng):
            last = None
            for (d, s) in pairs:
                last = eng.dma_start(out=d, in_=s,
                                     allow_slow_non_contiguous=slow)
                last.then_inc(self.sem[sem_name], 16)
            return (last, len(pairs))
        return self.task(engine, emit, waits, dma=sem_name, chain=chain)


def build(nc: bass.Bass, DBG=None, stop=99, skip_prep=False):
    DBG = DBG or {}
    dt = nc.dram_tensor
    x_tm = dt("x_tm", [TOK, E], F32, kind="ExternalInput")
    x_fmT = dt("x_fmT", [E, TOK], F32, kind="ExternalInput")
    x_haloT = dt("x_haloT", [E, 4], F32, kind="ExternalInput")
    x_halo = dt("x_halo", [4, E], F32, kind="ExternalInput")
    WQ = dt("Wq", [E, E], F32, kind="ExternalInput")
    WK = dt("Wk", [E, E], F32, kind="ExternalInput")
    WV = dt("Wv", [E, E], F32, kind="ExternalInput")
    WPQ = dt("Wphi_q", [E, E], F32, kind="ExternalInput")
    WPK = dt("Wphi_k", [E, E], F32, kind="ExternalInput")
    w2_slice = dt("w2_slice", [512, E], F32, kind="ExternalInput")
    wg_slice = dt("wg_slice", [128, INNER], F32, kind="ExternalInput")
    wu_slice = dt("wu_slice", [128, INNER], F32, kind="ExternalInput")
    wd_slice = dt("wd_slice", [512, E], F32, kind="ExternalInput")
    w1_slice = dt("w1_slice", [128, INNER], F32, kind="ExternalInput")
    ga_slice = dt("ga_slice", [128, 1], F32, kind="ExternalInput")
    convw_bd = dt("convw_bd", [FC, 5, 128, 128], F32, kind="ExternalInput")
    VEC_FM = ["bq_fm", "bk_fm", "gq_fm", "Bq_fm", "gk_fm", "Bk_fm",
              "ga_fm", "Ba_fm", "bphik_fm", "convb_fm"]
    vecs_fm = {nm: dt(nm, [128, FC], F32, kind="ExternalInput") for nm in VEC_FM}
    b1_fm = dt("b1_fm", [128, IC], F32, kind="ExternalInput")
    ROWS = ["bq_row", "bk_row", "bv_row", "bphiq_row", "bphik_row",
            "b2_row", "Ba_row"]
    rows_in = {nm: dt(nm, [1, E], F32, kind="ExternalInput") for nm in ROWS}
    gvB_in = dt("gvB", [128, E], F32, kind="ExternalInput")
    bvB_in = dt("bvB", [128, E], F32, kind="ExternalInput")
    gaB_in = dt("gaB", [128, E], F32, kind="ExternalInput")
    gammaB_in = dt("gammaB", [128, 1], F32, kind="ExternalInput")
    cmask_in = dt("cmask", [128, 4], F32, kind="ExternalInput")
    zpad_in = dt("zpad", [16, 1], F32, kind="ExternalInput")
    ident_in = dt("ident_in", [128, 128], BF16, kind="ExternalInput")
    R16_in = dt("R16_in", [16, E], BF16, kind="ExternalInput")
    onesrow_in = dt("onesrow_in", [1, HTOK], BF16, kind="ExternalInput")
    onescol_in = dt("onescol_in", [128, 1], BF16, kind="ExternalInput")
    ones16_in = dt("ones16_in", [1, 16], BF16, kind="ExternalInput")
    out = dt("out", [TOK, E], F32, kind="ExternalOutput")

    ag1_in = dt("ag1_in", [4], F32)
    ag1_out = dt("ag1_out", [4], F32)
    ag2_in = dt("ag2_in", [AG2_PER], BF16)
    ag2_out = dt("ag2_out", [NCORE, AG2_PER], BF16)
    ag3_in = dt("ag3_in", [E], F32)
    ag3_out = dt("ag3_out", [E], F32)
    ag4_in = dt("ag4_in", [32], F32)
    ag4_out = dt("ag4_out", [4, 32], F32)
    ag5_in = dt("ag5_in", [65 * 1024], F32)
    ag5_out = dt("ag5_out", [65 * 1024], F32)
    sp_outacc = dt("sp_outacc", [TOK, E], F32)
    sp_yk = dt("sp_yk", [FC, 128, TOK], BF16)
    sp_tvq = dt("sp_tvq", [FC, 128, HTOK], BF16)
    sp_tvk = dt("sp_tvk", [TC, 128, E], BF16)
    sp_vn = dt("sp_vn", [TC, 128, E], BF16)
    sp_xq = dt("sp_xq", [FC, 128, TOK], BF16)
    sp_guq = dt("sp_guq", [TC, 128, IC, 128], BF16)
    scal1 = dt("scal1", [4], F32)

    def ag2v(r, part):
        a = ag2_out.ap()[r]
        off = AG2_OFF[part]
        p = 512 if part in ("wd", "w2") else 128
        n = E if part in ("wd", "w2") else INNER
        return a[off:off + p * n].rearrange("(p n) -> p n", p=p)

    dbg_t = {nm: dt("dbg_" + nm, shp, F32, kind="ExternalOutput")
             for nm, shp in DBG.items()}

    es = ExitStack()
    SB = lambda n, shp, d=F32, **kw: es.enter_context(nc.sbuf_tensor(n, shp, d, **kw))
    PS = lambda n, shp: es.enter_context(nc.psum_tensor(n, shp, F32))

    with es:
        K = Bld(nc, es)
        K.tokbuf = SB("tokbuf", [1, 8], F32)

        ident = SB("ident", [128, 128], BF16)
        R16 = SB("R16", [16, E], BF16)
        onesrow = SB("onesrow", [1, HTOK], BF16)
        onescol = SB("onescol", [128, 1], BF16)
        ones16 = SB("ones16", [1, 16], BF16)
        c_cmask = SB("c_cmask", [128, 4], F32)
        c_zpad = SB("c_zpad", [16, 1], F32)
        c_gammaB = SB("c_gammaB", [128, 1], F32)
        c_gasl = SB("c_gasl", [128, 1], F32)
        v_fm = {nm: SB("v_" + nm, [128, FC], F32) for nm in VEC_FM}
        v_b1 = SB("v_b1", [128, IC], F32)
        v_b1p = SB("v_b1p", [128, IC], F32)
        c_gvB = SB("c_gvB", [128, E], BF16)
        c_bvB = SB("c_bvB", [128, E], BF16)
        c_gaI = SB("c_gaI", [128, E], BF16)
        c_gaB32 = SB("c_gaB32", [128, E], F32)
        st_mu = SB("st_mu", [128, TC], F32)
        st_rs = SB("st_rs", [128, TC], F32)
        st_s = SB("st_s", [128, TC], F32)
        st_r = SB("st_r", [128, TC], F32)
        st_rsk_tm = SB("st_rsk_tm", [128, TC], F32)
        lrow = SB("lrow", [1, 2080], F32)
        lrow2 = SB("lrow2", [1, 2080], F32)
        rowC = SB("rowC", [16, 2080], F32)
        rowD = SB("rowD", [16, 2080], F32)
        hm16 = SB("hm16", [16, 2080], BF16)
        rx2t = SB("rx2t", [2, 2080], BF16)
        y2 = SB("y2", [2, 2080], BF16)
        r_rsY = SB("r_rsY", [1, 2080], BF16)
        r_mkrk = SB("r_mkrk", [1, 2080], BF16)
        rx2h = SB("rx2h", [2, 4], BF16)
        rq2h = SB("rq2h", [2, 4], BF16)
        r_rsqh_bf = SB("r_rsqh_bf", [1, 4], BF16)
        exS = SB("exS", [2, E], BF16)
        exM = SB("exM", [2, E], BF16)
        rowtmp2 = SB("rowtmp2", [1, E], F32)
        rowtmp_bf = SB("rowtmp_bf", [1, E], BF16)
        b2pBa_bf = SB("b2pBa_bf", [1, E], BF16)
        bog = SB("bog", [128, FC], BF16)
        stat48 = SB("stat48", [128, 3 * TC], BF16)
        qacc = SB("qacc", [128, FC], F32)
        s0row = SB("s0row", [1, 4], F32)
        smallf = SB("smallf", [128, 16], F32)
        junk = SB("junk", [128, TOK], F32)
        junkb = SB("junkb", [128, E], BF16)
        sc_Mi = SB("sc_Mi", [16, 1], F32)
        sc_negM = SB("sc_negM", [16, 1], F32)
        sc_Zi = SB("sc_Zi", [16, 1], F32)
        sc_t16 = SB("sc_t16", [16, 16], F32)
        sc_Mall = SB("sc_Mall", [16, 4], F32)
        sc_Zall = SB("sc_Zall", [16, 4], F32)
        sc_Mc = SB("sc_Mc", [16, 1], F32)
        sc_Zc = SB("sc_Zc", [16, 1], F32)
        sc_fac = SB("sc_fac", [16, 1], F32)
        sc_a0 = SB("sc_a0", [16, 1], F32)
        a0_bf = SB("a0_bf", [16, 1], BF16)
        qg_fm = SB("qg_fm", [128, FC], F32)
        qgp_pad = SB("qgp_pad", [128, 16 * FC], BF16)
        qgb_pad = SB("qgb_pad", [128, 16 * FC], BF16)
        ks_pad = SB("ks_pad", [128, 16 * FC], BF16)
        r_hsum = SB("r_hsum", [1, 16], BF16)
        r_qgs = SB("r_qgs", [1, 16], BF16)
        co16 = SB("co16", [16, 1], F32)
        qgs16 = SB("qgs16", [16, 1], F32)
        hs16 = SB("hs16", [16, 1], F32)
        ksum_fm = SB("ksum_fm", [128, FC], F32)
        pk0_fm = SB("pk0_fm", [128, FC], F32)
        kvbd = SB("kvbd", [128, FC, 128], BF16)
        p_sw = SB("p_sw", [128, 8], F32)
        rV = SB("rV", [128, 4], F32)
        rV2 = SB("rV2", [128, 4], F32)
        p_absn = SB("p_absn", [128, 4], F32)
        agv = SB("agv", [128, 4], F32)
        gcb_fm = SB("gcb_fm", [128, FC], F32)
        b1r_fm = SB("b1r_fm", [128, IC], F32)
        WST = SB("WST", [128, 2, INNER], BF16)
        WFLAT = WST.ap().rearrange("p a n -> p (a n)")
        WARE = WFLAT.rearrange("p (a n) -> p a n", a=FC)
        stgA = SB("stgA", [128, 2, E], BF16)
        stgAF = stgA.ap().rearrange("p a n -> p (a n)")
        stgB = SB("stgB", [128, 2, E], BF16)
        stgBF = stgB.ap().rearrange("p a n -> p (a n)")
        stgC = SB("stgC", [128, 1, E], F32)
        fout = SB("fout", [128, E], F32)
        xq_fmt = SB("xq_fmt", [128, FC, 128], BF16, is_transpose_dest=True)

        pBig = PS("pBig", [128, 2048])
        pA = PS("pA", [128, 512])
        pB = PS("pB", [128, 512])
        pC = PS("pC", [128, 512])
        pD = PS("pD", [128, 512])

        pairs = [
            (ident[:, :], ident_in.ap()), (R16[:, :], R16_in.ap()),
            (onesrow[:, :], onesrow_in.ap()), (onescol[:, :], onescol_in.ap()),
            (ones16[:, :], ones16_in.ap()), (c_cmask[:, :], cmask_in.ap()),
            (c_zpad[:, :], zpad_in.ap()),
            (c_gammaB[:, :], gammaB_in.ap()), (c_gasl[:, :], ga_slice.ap()),
            (v_b1[:, :], b1_fm.ap()), (c_gaB32[:, :], gaB_in.ap()),
        ]
        pairs += [(v_fm[nm][:, :], vecs_fm[nm].ap()) for nm in VEC_FM]
        K.dmas("sync", "dma", pairs)
        K.dmas("gpsimd", "dmag", [(c_gvB[:, :], gvB_in.ap()),
                                  (c_bvB[:, :], bvB_in.ap())])


        def _dump(nm, src_ap, engine="sync"):
            if nm in DBG:
                q = "dma" if engine == "sync" else "dmag"
                K.dmas(engine, q, [(dbg_t[nm].ap(), src_ap)])

        def _dummy_fin():
            K.dmas("sync", "dma", [(out.ap()[0:128, :], fout[:, :])])
            K.task("sync", lambda q: None)
        if stop <= 1:
            _dummy_fin()
            return nc
        stg32_g = nc.sbuf_tensor("stg32", [128, 2, INNER], F32)
        stg32 = stg32_g.__enter__()
        if not skip_prep:
            K.dmas("sync", "dma", [(stg32[:, 0, :], wg_slice.ap()),
                                   (stg32[:, 1, :], wu_slice.ap())])

            def abs1(v):
                v.tensor_reduce(p_absn[:, 0:1], stg32[:, 0, :], AX.X, ALU.add,
                                apply_absolute_value=True)
                return v.tensor_reduce(p_absn[:, 1:2], stg32[:, 1, :], AX.X, ALU.add,
                                       apply_absolute_value=True)
            K.task("vector", abs1)
            K.dmas("sync", "dma",
                   [(stg32[:, 0, i * E:(i + 1) * E],
                     wd_slice.ap()[128 * i:128 * (i + 1), :]) for i in range(4)])
            K.task("vector", lambda v: v.tensor_reduce(
                p_absn[:, 2:3], stg32[:, 0, :], AX.X, ALU.add,
                apply_absolute_value=True))
            K.dmas("sync", "dma",
                   [(lrow[0:1, 128 * k:128 * (k + 1)], p_absn[:, k:k + 1])
                    for k in range(3)], slow=True)

            def absum(v):
                for k in range(3):
                    v.tensor_reduce(lrow[0:1, 512 + k:513 + k],
                                    lrow[0:1, 128 * k:128 * (k + 1)], AX.X, ALU.add)
                return None
            K.task("vector", absum)
            K.dmas("gpsimd", "dmag", [(ag1_in.ap()[0:3], lrow[0:1, 512:515])])

            def ag1_go(g):
                return g.collective_compute(
                    "AllReduce", ALU.add, replica_groups=[list(range(NCORE))],
                    ins=[ag1_in.ap().opt()], outs=[ag1_out.ap().opt()])
            K.task("gpsimd", ag1_go)
            K.dmas("gpsimd", "dmag",
                   [(agv[:, 0:4],
                     ag1_out.ap().rearrange("(a n) -> a n", a=1)
                     .broadcast_to((128, 4)))])

            def sw_calc(v):
                v.tensor_scalar(p_sw[:, 0:4], agv[:, 0:4], 1.0 / (E * INNER), 1e-5,
                                ALU.mult, ALU.max)
                v.reciprocal(rV[:, :], p_sw[:, 0:4])
                return v.tensor_copy(p_sw[:, 4:8], rV[:, :])
            K.task("vector", sw_calc)

            K.dmas("sync", "dma", [(stg32[:, 0, :], wg_slice.ap()),
                                   (stg32[:, 1, :], wu_slice.ap())])

            def tern_gu(v):
                for i in range(2):
                    v.tensor_scalar(stg32[:, i, :], stg32[:, i, :],
                                    p_sw[:, 4 + i:5 + i], CROUND, ALU.mult, ALU.add)
                    v.tensor_scalar(stg32[:, i, :], stg32[:, i, :], -CROUND, None,
                                    ALU.add)
                    v.tensor_scalar(WST[:, i, :], stg32[:, i, :], -1.0, 1.0,
                                    ALU.max, ALU.min)
                return None
            K.task("vector", tern_gu)
            K.dmas("sync", "dma", [
                (ag2_in.ap()[AG2_OFF["wg"]:AG2_OFF["wg"] + AG2_WG]
                 .rearrange("(p n) -> p n", p=128), WST[:, 0, :]),
                (ag2_in.ap()[AG2_OFF["wu"]:AG2_OFF["wu"] + AG2_WU]
                 .rearrange("(p n) -> p n", p=128), WST[:, 1, :]),
            ])
            K.dmas("sync", "dma",
                   [(stg32[:, 0, i * E:(i + 1) * E],
                     wd_slice.ap()[128 * i:128 * (i + 1), :]) for i in range(4)])

            def tern_wd(v):
                v.tensor_scalar(stg32[:, 0, :], stg32[:, 0, :], p_sw[:, 6:7],
                                CROUND, ALU.mult, ALU.add)
                v.tensor_scalar(stg32[:, 0, :], stg32[:, 0, :], -CROUND, None,
                                ALU.add)
                return v.tensor_scalar(WST[:, 0, :], stg32[:, 0, :], -1.0, 1.0,
                                       ALU.max, ALU.min)
            K.task("vector", tern_wd)
            K.dmas("sync", "dma", [
                (ag2_in.ap()[AG2_OFF["wd"]:AG2_OFF["wd"] + AG2_WD]
                 .rearrange("(a p n) -> p a n", p=128, a=4),
                 WST[:, 0, :].rearrange("p (a n) -> p a n", a=4)),
            ])
            K.dmas("gpsimd", "dmag", [
                (ag2_in.ap()[AG2_OFF["w2"]:AG2_OFF["w2"] + AG2_W2],
                 w2_slice.ap().rearrange("p n -> (p n)")),
            ])
            K.dmas("sync", "dma", [(stg32[:, 0, :], w1_slice.ap())])
            K.task("vector", lambda v: v.tensor_scalar(
                WST[:, 0, :], stg32[:, 0, :], c_gasl[:, 0:1], None, ALU.mult))
            K.dmas("sync", "dma", [
                (ag2_in.ap()[AG2_OFF["w1"]:AG2_OFF["w1"] + AG2_W1]
                 .rearrange("(p n) -> p n", p=128), WST[:, 0, :]),
            ])

            if stop <= 2:
                stg32_g.__exit__(None, None, None)
                _dummy_fin()
                return nc

            def ag2_go(g):
                return g.collective_compute(
                    "AllGather", ALU.bypass, replica_groups=[list(range(NCORE))],
                    ins=[ag2_in.ap().opt()], outs=[ag2_out.ap().opt()])
            t_ag2 = K.task("gpsimd", ag2_go)
        else:
            t_ag2 = ("gpsimd", 0)
        if stop <= 3:
            stg32_g.__exit__(None, None, None)
            _dummy_fin()
            return nc
        xdb = stg32.ap().rearrange("p a n -> p (a n)").rearrange(
            "p (a n) -> p a n", a=8)
        for c in range(TC):
            K.dmas("sync", "dma",
                   [(xdb[:, c % 2, :], x_tm.ap()[128 * c:128 * (c + 1), :])])

            def stats1(v, c=c):
                xc = xdb[:, c % 2, :]
                v.tensor_reduce(rV[:, 0:1], xc, AX.X, ALU.add)
                v.tensor_scalar(st_mu[:, c:c + 1], rV[:, 0:1], 1.0 / E, None,
                                ALU.mult)
                v.tensor_tensor(junk[:, 0:E], xc, xc, ALU.mult)
                v.tensor_reduce(rV[:, 1:2], junk[:, 0:E], AX.X, ALU.add)
                v.tensor_scalar(rV[:, 1:2], rV[:, 1:2], 1.0 / E, None, ALU.mult)
                v.tensor_tensor(rV[:, 2:3], st_mu[:, c:c + 1],
                                st_mu[:, c:c + 1], ALU.mult)
                v.tensor_tensor(rV[:, 1:2], rV[:, 1:2], rV[:, 2:3],
                                ALU.subtract)
                return v.tensor_scalar(st_rs[:, c:c + 1], rV[:, 1:2], EPS_LN,
                                       None, ALU.add)
            K.task("vector", stats1)
        K.task("scalar", lambda s: s.activation(st_rs[:, :], st_rs[:, :],
                                                AF.Sqrt))

        def rs_fin(v):
            v.reciprocal(junk[:, 0:TC], st_rs[:, :])
            return v.tensor_copy(st_rs[:, :], junk[:, 0:TC])
        K.task("vector", rs_fin)
        for c in range(TC):
            K.dmas("sync", "dma",
                   [(xdb[:, c % 2, :], x_tm.ap()[128 * c:128 * (c + 1), :])])

            def stats3(v, c=c):
                xc = xdb[:, c % 2, :]
                v.tensor_scalar(junk[:, 0:E], xc, st_mu[:, c:c + 1], None,
                                ALU.subtract)
                v.tensor_reduce(rV[:, 0:1], junk[:, 0:E], AX.X, ALU.max,
                                apply_absolute_value=True)
                v.tensor_tensor(rV[:, 0:1], rV[:, 0:1], st_rs[:, c:c + 1],
                                ALU.mult)
                v.tensor_scalar(st_r[:, c:c + 1], rV[:, 0:1], 1e-5,
                                1.0 / 127.0, ALU.max, ALU.mult)
                v.tensor_copy(rV[:, 1:2], st_r[:, c:c + 1])
                v.reciprocal(rV2[:, 0:1], rV[:, 1:2])
                v.tensor_copy(st_s[:, c:c + 1], rV2[:, 0:1])
                v.tensor_scalar(rV[:, 2:3], st_rs[:, c:c + 1], 1.0, None,
                                ALU.add)
                v.tensor_tensor(rV[:, 3:4], st_mu[:, c:c + 1],
                                st_rs[:, c:c + 1], ALU.mult)
                v.tensor_scalar(rV[:, 3:4], rV[:, 3:4], -1.0, None, ALU.mult)
                v.tensor_scalar(stgC[:, 0, :], xc, rV[:, 2:3], rV[:, 3:4],
                                ALU.mult, ALU.add)
                v.tensor_tensor(rV2[:, 1:2], st_rs[:, c:c + 1],
                                rV2[:, 0:1], ALU.mult)
                v.tensor_tensor(rV2[:, 2:3], rV[:, 3:4], rV2[:, 0:1], ALU.mult)
                v.tensor_scalar(junk[:, 0:E], xc, rV2[:, 1:2], rV2[:, 2:3],
                                ALU.mult, ALU.add)
                v.tensor_scalar(junk[:, 0:E], junk[:, 0:E], CROUND, -CROUND,
                                ALU.add, ALU.add)
                return v.tensor_copy(junkb[:, :], junk[:, 0:E])
            K.task("vector", stats3)
            K.dmas("sync", "dma",
                   [(sp_outacc.ap()[128 * c:128 * (c + 1), :], stgC[:, 0, :])])
            K.task("scalar", lambda s, c=c:
                   (s.dma_start_transpose(xq_fmt[:, :, :], junkb[:, :])
                    .then_inc(K.sem["dmas"], 16), 1), dma="dmas")
            K.dmas("sync", "dma",
                   [(sp_xq.ap()[j, :, 128 * c:128 * (c + 1)], xq_fmt[:, j, :])
                    for j in range(FC)])

        def pack48(v):
            v.tensor_scalar(stat48[:, 0:TC], st_mu[:, :], -1.0, None, ALU.mult)
            v.reciprocal(junk[:, 0:TC], st_rs[:, :])
            v.tensor_copy(stat48[:, TC:2 * TC], junk[:, 0:TC])
            return v.tensor_copy(stat48[:, 2 * TC:3 * TC], st_rs[:, :])
        K.task("vector", pack48)
        K.dmas("sync", "dma",
               [(rx2t[0:1, 128 * c:128 * (c + 1)], stat48[:, c:c + 1])
                for c in range(TC)] +
               [(rx2t[1:2, 128 * c:128 * (c + 1)], stat48[:, TC + c:TC + c + 1])
                for c in range(TC)])

        K.dmas("sync", "dma", [(xdb[0:4, 4, :], x_halo.ap())])

        def hstats(v):
            xh = xdb[0:4, 4, :]
            v.tensor_scalar(junk[0:4, 0:E], xh, 1.0, 0.0, ALU.mult, ALU.add,
                            accum_out=smallf[0:4, 0:1])
            v.tensor_scalar(smallf[0:4, 0:1], smallf[0:4, 0:1], 1.0 / E, None,
                            ALU.mult)
            v.scalar_tensor_tensor(junk[0:4, 0:E], xh, 1.0, xh, ALU.mult,
                                   ALU.mult, accum_out=smallf[0:4, 1:2])
            v.tensor_scalar(smallf[0:4, 1:2], smallf[0:4, 1:2], 1.0 / E, None,
                            ALU.mult)
            v.tensor_tensor(smallf[0:4, 2:3], smallf[0:4, 0:1],
                            smallf[0:4, 0:1], ALU.mult)
            v.tensor_tensor(smallf[0:4, 1:2], smallf[0:4, 1:2],
                            smallf[0:4, 2:3], ALU.subtract)
            return v.tensor_scalar(smallf[0:4, 1:2], smallf[0:4, 1:2], EPS_LN,
                                   None, ALU.add)
        K.task("vector", hstats)
        K.task("scalar", lambda s: s.activation(smallf[0:4, 1:2],
                                                smallf[0:4, 1:2], AF.Sqrt))

        def hstats2(v):
            v.reciprocal(smallf[0:4, 2:3], smallf[0:4, 1:2])
            v.tensor_scalar(smallf[0:4, 3:4], smallf[0:4, 0:1], -1.0, None,
                            ALU.mult)
            return None
        K.task("vector", hstats2)
        K.dmas("gpsimd", "dmag", [
            (rx2h[0:1, :].rearrange("a (c p) -> a p c", p=4), smallf[0:4, 3:4]),
            (rx2h[1:2, :].rearrange("a (c p) -> a p c", p=4), smallf[0:4, 1:2]),
        ], slow=True)
        _dump("outacc", sp_outacc.ap())
        _dump("xq0", sp_xq.ap()[0], "gpsimd")
        _dump("stmu", st_mu[:, :])
        _dump("strs", st_rs[:, :])
        _dump("st48", stat48[:, :], "gpsimd")
        _dump("rx2d", rx2t[0:2, :], "gpsimd")
        stg32_g.__exit__(None, None, None)

        if stop <= 4:
            _dummy_fin()
            return nc
        with nc.sbuf_tensor("xfm", [128, FC, HTOK], BF16) as xfm, \
             nc.sbuf_tensor("ybig", [128, FC, TOK], BF16) as ybig, \
             nc.sbuf_tensor("yh", [128, FC * 4], BF16) as yh, \
             nc.sbuf_tensor("rsqB", [128, TOK], BF16) as rsqB:

            K.dmas("gpsimd", "dmag",
                   [(xfm[:, j, 0:TOK], x_fmT.ap()[128 * j:128 * (j + 1), :])
                    for j in range(FC)] +
                   [(xfm[:, j, TOK:TOK + 4],
                     x_haloT.ap()[128 * j:128 * (j + 1), :])
                    for j in range(FC)])

            if stop == 401:
                _dummy_fin()
                return nc
            rx2 = rx2t
            r_rsY_bf = r_rsY[0:1, 0:TOK]
            r_mukrsk_bf = r_mkrk[0:1, 0:TOK]

            def proj_QK(W, brow_dram, spill_y):
                K.dmas("gpsimd", "dmag",
                       [(WARE[:, k, :], W.ap()[128 * k:128 * (k + 1), :])
                        for k in range(FC)])
                K.dmas("gpsimd", "dmag", [(rowtmp_bf[:, :], brow_dram.ap())])
                if stop == 402:
                    _dummy_fin()
                    return True

                def wsum(t):
                    for k in range(FC):
                        for n in range(2):
                            t.matmul(pBig[0:1, 512 * n:512 * (n + 1)],
                                     onescol[:, :],
                                     WARE[:, k, 512 * n:512 * (n + 1)],
                                     start=(k == 0), stop=(k == FC - 1))
                    return None
                K.task("tensor", wsum)
                if stop == 403:
                    _dummy_fin()
                    return True

                def exb(v):
                    for n in range(2):
                        v.tensor_copy(exS[0:1, 512 * n:512 * (n + 1)],
                                      pBig[0:1, 512 * n:512 * (n + 1)])
                    return None
                K.task("vector", exb)
                if stop == 4035:
                    _dummy_fin()
                    return True
                K.dmas("sync", "dma", [(exS[1:2, :], rowtmp_bf[0:1, :])])
                if stop == 404:
                    _dummy_fin()
                    return True

                for j in range(FC):
                    def main(t, j=j):
                        for k in range(FC):
                            for n in range(4):
                                t.matmul(pBig[:, 512 * n:512 * (n + 1)],
                                         WARE[:, k, 128 * j:128 * (j + 1)],
                                         xfm[:, k, 512 * n:512 * (n + 1)],
                                         start=(k == 0), stop=False)
                            t.matmul(pA[:, 0:4],
                                     WARE[:, k, 128 * j:128 * (j + 1)],
                                     xfm[:, k, TOK:TOK + 4],
                                     start=(k == 0), stop=False)
                        for n in range(4):
                            t.matmul(pBig[:, 512 * n:512 * (n + 1)],
                                     exS[:, 128 * j:128 * (j + 1)],
                                     rx2[:, 512 * n:512 * (n + 1)],
                                     start=False, stop=True)
                        return t.matmul(pA[:, 0:4], exS[:, 128 * j:128 * (j + 1)],
                                        rx2h[:, :], start=False, stop=True)
                    K.task("tensor", main)

                    def evj(s, j=j):
                        s.copy(ybig[:, j, :], pBig[:, 0:TOK])
                        return s.copy(yh[:, 4 * j:4 * (j + 1)], pA[:, 0:4])
                    K.task("scalar", evj)
                    if spill_y:
                        K.dmas("sync", "dma",
                               [(sp_yk.ap()[j, :, :], ybig[:, j, :])])
                    if stop == 405 + j:
                        _dummy_fin()
                        return True

            def stats_of_y(r2, rs_bf, rs32, mu_row, murs_scr, murs_bf, r2h,
                           rsh_bf, tm_out):
                def smm(t):
                    for j in range(FC):
                        for n in range(4):
                            t.matmul(pBig[0:1, 512 * n:512 * (n + 1)],
                                     onescol[:, :],
                                     ybig[:, j, 512 * n:512 * (n + 1)],
                                     start=(j == 0), stop=(j == FC - 1))
                        t.matmul(pA[0:1, 0:4], onescol[:, :],
                                 yh[:, 4 * j:4 * (j + 1)],
                                 start=(j == 0), stop=(j == FC - 1))
                    return None
                K.task("tensor", smm)

                def sev(s):
                    s.copy(mu_row[0:1, 0:TOK], pBig[0:1, 0:TOK])
                    return s.copy(s0row[:, :], pA[0:1, 0:4])
                K.task("scalar", sev)

                for j in range(FC):
                    K.task("scalar", lambda s, j=j: s.activation(
                        stgBF[:, :], ybig[:, j, :], AF.Square))

                    def sqm(t, j=j):
                        for n in range(4):
                            t.matmul(pBig[0:1, 512 * n:512 * (n + 1)],
                                     onescol[:, :],
                                     stgBF[:, 512 * n:512 * (n + 1)],
                                     start=(j == 0), stop=(j == FC - 1))
                        return None
                    K.task("tensor", sqm)
                K.task("scalar", lambda s: s.activation(stgBF[:, 0:FC * 4],
                                                        yh[:, :], AF.Square))

                def sqhm(t):
                    for j in range(FC):
                        t.matmul(pA[0:1, 4:8], onescol[:, :],
                                 stgBF[:, 4 * j:4 * (j + 1)],
                                 start=(j == 0), stop=(j == FC - 1))
                    return None
                K.task("tensor", sqhm)

                def rows1(v):
                    v.tensor_scalar(mu_row[0:1, 0:TOK], mu_row[0:1, 0:TOK],
                                    1.0 / E, None, ALU.mult)
                    v.tensor_scalar(lrow2[0:1, 0:TOK], pBig[0:1, 0:TOK],
                                    1.0 / E, None, ALU.mult)
                    v.tensor_tensor(murs_scr[0:1, 0:TOK], mu_row[0:1, 0:TOK],
                                    mu_row[0:1, 0:TOK], ALU.mult)
                    v.tensor_tensor(lrow2[0:1, 0:TOK], lrow2[0:1, 0:TOK],
                                    murs_scr[0:1, 0:TOK], ALU.subtract)
                    v.tensor_scalar(lrow2[0:1, 0:TOK], lrow2[0:1, 0:TOK],
                                    EPS_LN, None, ALU.add)
                    v.tensor_scalar(s0row[:, 0:4], s0row[:, 0:4], 1.0 / E, None,
                                    ALU.mult)
                    v.tensor_scalar(smallf[0:1, 4:8], pA[0:1, 4:8], 1.0 / E,
                                    None, ALU.mult)
                    v.tensor_tensor(smallf[0:1, 8:12], s0row[0:1, 0:4],
                                    s0row[0:1, 0:4], ALU.mult)
                    v.tensor_tensor(smallf[0:1, 4:8], smallf[0:1, 4:8],
                                    smallf[0:1, 8:12], ALU.subtract)
                    return v.tensor_scalar(smallf[0:1, 4:8], smallf[0:1, 4:8],
                                           EPS_LN, None, ALU.add)
                K.task("vector", rows1)

                def rows_s(s):
                    s.activation(lrow2[0:1, 0:TOK], lrow2[0:1, 0:TOK], AF.Sqrt)
                    return s.activation(smallf[0:1, 4:8], smallf[0:1, 4:8],
                                        AF.Sqrt)
                K.task("scalar", rows_s)

                def rows2(v):
                    v.reciprocal(rs32[0:1, 0:TOK], lrow2[0:1, 0:TOK])
                    v.tensor_copy(rs_bf, rs32[0:1, 0:TOK])
                    if murs_bf is not None:
                        v.tensor_tensor(murs_scr[0:1, 0:TOK], mu_row[0:1, 0:TOK],
                                        rs32[0:1, 0:TOK], ALU.mult)
                        v.tensor_copy(murs_bf, murs_scr[0:1, 0:TOK])
                    v.tensor_scalar(r2[0:1, 0:TOK], mu_row[0:1, 0:TOK], -1.0,
                                    None, ALU.mult)
                    v.tensor_scalar(r2h[0:1, :], s0row[0:1, 0:4], -1.0, None,
                                    ALU.mult)
                    v.tensor_copy(rV[0:1, :], smallf[0:1, 4:8])
                    v.reciprocal(rV2[0:1, :], rV[0:1, :])
                    return v.tensor_copy(smallf[0:1, 8:12], rV2[0:1, :])
                K.task("vector", rows2)
                K.dmas("gpsimd", "dmag", [
                    (r2[1:2, 0:TOK], lrow2[0:1, 0:TOK]),
                    (r2h[1:2, :], smallf[0:1, 4:8]),
                    (rsh_bf[0:1, :], smallf[0:1, 8:12]),
                ])
                if tm_out is not None:
                    K.dmas("sync", "dma",
                           [(tm_out[:, c:c + 1],
                             rs32[0:1, 128 * c:128 * (c + 1)])
                            for c in range(TC)], slow=True)

            if proj_QK(WQ, rows_in["bq_row"], spill_y=False):
                return nc
            if stop == 41:
                _dummy_fin()
                return nc
            stats_of_y(y2, r_rsY_bf, rowD[0:1], lrow, rowC[0:1], None, rq2h,
                       r_rsqh_bf, None)

            _dump("yq0", ybig[:, 0, :], "gpsimd")
            if "qstat" in DBG:
                K.dmas("sync", "dma", [(dbg_t["qstat"].ap()[0:1, :],
                                        lrow[0:1, 0:TOK]),
                                       (dbg_t["qstat"].ap()[1:2, :],
                                        rowD[0:1, 0:TOK])])
            if stop == 42:
                _dummy_fin()
                return nc

            def rsqB_mm(t):
                for n in range(4):
                    t.matmul(pBig[:, 512 * n:512 * (n + 1)], onesrow[0:1, 0:128],
                             r_rsY[0:1, 512 * n:512 * (n + 1)],
                             start=True, stop=True)
                return None
            K.task("tensor", rsqB_mm)
            K.task("scalar", lambda s: s.copy(rsqB[:, :], pBig[:, 0:TOK]))

            def qsum_acc(v):
                for j in range(FC):
                    v.tensor_tensor(junk[:, 0:TOK], ybig[:, j, :], rsqB[:, :],
                                    ALU.mult)
                    v.tensor_reduce(qacc[:, j:j + 1], junk[:, 0:TOK], AX.X,
                                    ALU.add)
                v.tensor_tensor(junk[0:1, 0:TOK], lrow[0:1, 0:TOK],
                                rowD[0:1, 0:TOK], ALU.mult)
                return v.tensor_reduce(s0row[:, 0:1], junk[0:1, 0:TOK], AX.X,
                                       ALU.add)
            K.task("vector", qsum_acc)
            K.dmas("sync", "dma", [(scal1.ap()[0:1], s0row[0:1, 0:1])])
            K.dmas("sync", "dma",
                   [(smallf[:, 15:16],
                     scal1.ap()[0:1].rearrange("(a n) -> a n", a=1)
                     .broadcast_to((128, 1)))])

            def qsum_acc2(v):
                v.tensor_scalar(qacc[:, :], qacc[:, :], smallf[:, 15:16], None,
                                ALU.subtract)
                v.tensor_tensor(qacc[:, :], qacc[:, :], v_fm["gq_fm"][:, :],
                                ALU.mult)
                return v.scalar_tensor_tensor(qacc[:, :], v_fm["Bq_fm"][:, :],
                                              float(TOK), qacc[:, :], ALU.mult,
                                              ALU.add)
            K.task("vector", qsum_acc2)
            _dump("qacc", qacc[:, :])
            K.dmas("gpsimd", "dmag",
                   [(ag3_in.ap().rearrange("(p n) -> p n", p=128), qacc[:, :])])

            def ag3_go(g):
                return g.collective_compute(
                    "AllReduce", ALU.add,
                    replica_groups=[[0, 1, 2, 3], [4, 5, 6, 7]],
                    ins=[ag3_in.ap().opt()], outs=[ag3_out.ap().opt()])
            t_ag3 = K.task("gpsimd", ag3_go)

            if stop == 43:
                _dummy_fin()
                return nc
            K.dmas("gpsimd", "dmag",
                   [(WARE[:, k, :], WPQ.ap()[128 * k:128 * (k + 1), :])
                    for k in range(FC)])

            def wpq_scale(v):
                for k in range(FC):
                    v.tensor_scalar(WARE[:, k, :], WARE[:, k, :],
                                    v_fm["gq_fm"][:, k:k + 1], None, ALU.mult)
                return None
            K.task("vector", wpq_scale)

            def wpq_sum(t):
                for k in range(FC):
                    for n in range(2):
                        t.matmul(pBig[0:1, 512 * n:512 * (n + 1)], onescol[:, :],
                                 WARE[:, k, 512 * n:512 * (n + 1)],
                                 start=(k == 0), stop=(k == FC - 1))
                return None
            K.task("tensor", wpq_sum)
            K.task("scalar", lambda s: s.copy(exS[0:1, :], pBig[0:1, 0:E]))

            def bog_b(v):
                v.reciprocal(smallf[:, 0:FC], v_fm["gq_fm"][:, :])
                v.tensor_tensor(smallf[:, 0:FC], v_fm["Bq_fm"][:, :],
                                smallf[:, 0:FC], ALU.mult)
                return v.tensor_copy(bog[:, :], smallf[:, 0:FC])
            K.task("vector", bog_b)

            def bphi_mm(t):
                for k in range(FC):
                    for n in range(2):
                        t.matmul(pBig[0:1, 1024 + 512 * n:1024 + 512 * (n + 1)],
                                 bog[:, k:k + 1],
                                 WARE[:, k, 512 * n:512 * (n + 1)],
                                 start=(k == 0), stop=(k == FC - 1))
                return None
            K.task("tensor", bphi_mm)
            K.dmas("gpsimd", "dmag",
                   [(rowtmp2[0:1, :], rows_in["bphiq_row"].ap())])

            def bphi_v(v):
                for n in range(2):
                    v.tensor_tensor(lrow[0:1, 512 * n:512 * (n + 1)],
                                    rowtmp2[0:1, 512 * n:512 * (n + 1)],
                                    pBig[0:1, 1024 + 512 * n:
                                         1024 + 512 * (n + 1)], ALU.add)
                return None
            K.task("vector", bphi_v)
            K.dmas("gpsimd", "dmag", [(exS[1:2, :], lrow[0:1, 0:E])])

            if stop == 44:
                _dummy_fin()
                return nc

            def rsqhB_mm(t):
                return t.matmul(pA[:, 4:8], onesrow[0:1, 0:128],
                                r_rsqh_bf[0:1, 0:4], start=True, stop=True)
            K.task("tensor", rsqhB_mm)
            K.task("vector", lambda v: v.tensor_copy(smallf[:, 8:12],
                                                     pA[:, 4:8]))

            for j in range(FC):
                def pq_mm(t, j=j):
                    for k in range(FC):
                        for n in range(4):
                            t.matmul(pBig[:, 512 * n:512 * (n + 1)],
                                     WARE[:, k, 128 * j:128 * (j + 1)],
                                     ybig[:, k, 512 * n:512 * (n + 1)],
                                     start=(k == 0), stop=False)
                        t.matmul(pA[:, 0:4], WARE[:, k, 128 * j:128 * (j + 1)],
                                 yh[:, 4 * k:4 * (k + 1)], start=(k == 0),
                                 stop=False)
                    for n in range(4):
                        t.matmul(pBig[:, 512 * n:512 * (n + 1)],
                                 exS[:, 128 * j:128 * (j + 1)],
                                 y2[:, 512 * n:512 * (n + 1)],
                                 start=False, stop=True)
                    return t.matmul(pA[:, 0:4], exS[:, 128 * j:128 * (j + 1)],
                                    rq2h[:, :], start=False, stop=True)
                K.task("tensor", pq_mm)

                def pq_v(v, j=j):
                    for n in range(4):
                        v.tensor_tensor(junk[:, 512 * n:512 * (n + 1)],
                                        pBig[:, 512 * n:512 * (n + 1)],
                                        rsqB[:, 512 * n:512 * (n + 1)],
                                        ALU.mult)
                    return v.tensor_tensor(smallf[:, 4:8], pA[:, 0:4],
                                           smallf[:, 8:12], ALU.mult)
                K.task("vector", pq_v)

                def pq_tanh(s, j=j):
                    s.activation(stgAF[:, 0:TOK], junk[:, 0:TOK], AF.Tanh)
                    return s.activation(junkb[:, 0:4], smallf[:, 4:8], AF.Tanh)
                K.task("scalar", pq_tanh)
                K.dmas("sync", "dma", [
                    (sp_tvq.ap()[j, :, 2:2 + TOK], stgAF[:, 0:TOK]),
                    (sp_tvq.ap()[j, :, 0:2], junkb[:, 0:2]),
                    (sp_tvq.ap()[j, :, 2 + TOK:4 + TOK], junkb[:, 2:4]),
                ])

            _dump("tvq0", sp_tvq.ap()[0], "gpsimd")
            if stop <= 5:
                _dummy_fin()
                return nc
            proj_QK(WK, rows_in["bk_row"], spill_y=True)
            stats_of_y(y2, r_rsY_bf, rowC[0:1], lrow, rowD[0:1], r_mukrsk_bf, rq2h,
                       r_rsqh_bf, st_rsk_tm)

            _dump("yk0", sp_yk.ap()[0], "gpsimd")
            if "kstat" in DBG:
                K.dmas("sync", "dma", [(dbg_t["kstat"].ap()[0:1, :],
                                        lrow[0:1, 0:TOK]),
                                       (dbg_t["kstat"].ap()[1:2, :],
                                        rowC[0:1, 0:TOK])])
            K.dmas("gpsimd", "dmag",
                   [(WARE[:, k, :], WPK.ap()[128 * k:128 * (k + 1), :])
                    for k in range(FC)])

            def wpk_scale(v):
                for k in range(FC):
                    v.tensor_scalar(WARE[:, k, :], WARE[:, k, :],
                                    v_fm["gk_fm"][:, k:k + 1], None, ALU.mult)
                return None
            K.task("vector", wpk_scale)

            def wpk_sum(t):
                for k in range(FC):
                    for n in range(2):
                        t.matmul(pBig[0:1, 512 * n:512 * (n + 1)], onescol[:, :],
                                 WARE[:, k, 512 * n:512 * (n + 1)],
                                 start=(k == 0), stop=(k == FC - 1))
                return None
            K.task("tensor", wpk_sum)
            K.task("scalar", lambda s: s.copy(exM[0:1, :], pBig[0:1, 0:E]))

            def bogk_b(v):
                v.reciprocal(smallf[:, 0:FC], v_fm["gk_fm"][:, :])
                v.tensor_tensor(smallf[:, 0:FC], v_fm["Bk_fm"][:, :],
                                smallf[:, 0:FC], ALU.mult)
                return v.tensor_copy(bog[:, :], smallf[:, 0:FC])
            K.task("vector", bogk_b)

            def bphik_mm(t):
                for k in range(FC):
                    for n in range(2):
                        t.matmul(pBig[0:1, 1024 + 512 * n:1024 + 512 * (n + 1)],
                                 bog[:, k:k + 1],
                                 WARE[:, k, 512 * n:512 * (n + 1)],
                                 start=(k == 0), stop=(k == FC - 1))
                return None
            K.task("tensor", bphik_mm)
            K.dmas("gpsimd", "dmag",
                   [(rowtmp2[0:1, :], rows_in["bphik_row"].ap())])

            def bphik_v(v):
                for n in range(2):
                    v.tensor_tensor(lrow[0:1, 512 * n:512 * (n + 1)],
                                    rowtmp2[0:1, 512 * n:512 * (n + 1)],
                                    pBig[0:1, 1024 + 512 * n:
                                         1024 + 512 * (n + 1)], ALU.add)
                return None
            K.task("vector", bphik_v)
            K.dmas("gpsimd", "dmag", [(exM[1:2, :], lrow[0:1, 0:E])])

            for c in range(TC):
                def pk_mm(t, c=c):
                    for k in range(FC):
                        for n in range(2):
                            t.matmul(pBig[:, 512 * n:512 * (n + 1)],
                                     ybig[:, k, 128 * c:128 * (c + 1)],
                                     WARE[:, k, 512 * n:512 * (n + 1)],
                                     start=(k == 0), stop=False)
                    for n in range(2):
                        t.matmul(pBig[:, 512 * n:512 * (n + 1)],
                                 y2[:, 128 * c:128 * (c + 1)],
                                 exM[:, 512 * n:512 * (n + 1)],
                                 start=False, stop=True)
                    return None
                K.task("tensor", pk_mm)
                K.task("scalar", lambda s, c=c: s.activation(
                    stgA[:, c % 2, :], pBig[:, 0:E], AF.Tanh,
                    scale=st_rsk_tm[:, c:c + 1]))
                K.dmas("sync", "dma", [(sp_tvk.ap()[c], stgA[:, c % 2, :])])

            if stop <= 6:
                _dummy_fin()
                return nc
            K.dmas("gpsimd", "dmag",
                   [(WARE[:, k, :], WV.ap()[128 * k:128 * (k + 1), :])
                    for k in range(FC)])

            def wv_sum(t):
                for k in range(FC):
                    for n in range(2):
                        t.matmul(pBig[0:1, 512 * n:512 * (n + 1)], onescol[:, :],
                                 WARE[:, k, 512 * n:512 * (n + 1)],
                                 start=(k == 0), stop=(k == FC - 1))
                return None
            K.task("tensor", wv_sum)
            K.dmas("gpsimd", "dmag", [(rowtmp_bf[:, :], rows_in["bv_row"].ap())])

            def wv_exb(v):
                for n in range(2):
                    v.tensor_copy(exM[0:1, 512 * n:512 * (n + 1)],
                                  pBig[0:1, 512 * n:512 * (n + 1)])
                return None
            K.task("vector", wv_exb)
            K.dmas("sync", "dma", [(exM[1:2, :], rowtmp_bf[0:1, :])])

            for c in range(TC):
                def v_mm(t, c=c):
                    for k in range(FC):
                        for n in range(2):
                            t.matmul(pBig[:, 512 * n:512 * (n + 1)],
                                     xfm[:, k, 128 * c:128 * (c + 1)],
                                     WARE[:, k, 512 * n:512 * (n + 1)],
                                     start=(k == 0), stop=False)
                    for n in range(2):
                        t.matmul(pBig[:, 512 * n:512 * (n + 1)],
                                 rx2[:, 128 * c:128 * (c + 1)],
                                 exM[:, 512 * n:512 * (n + 1)],
                                 start=False, stop=True)
                    return None
                K.task("tensor", v_mm)

                def v_st(s, c=c):
                    s.activation(junkb[:, :], pBig[:, 0:E], AF.Identity)
                    return s.activation(stgBF[:, 0:E], pBig[:, 0:E], AF.Square)
                K.task("scalar", v_st)

                def v_rows(v, c=c):
                    v.tensor_reduce(rV[:, 0:1], junkb[:, :], AX.X, ALU.add)
                    v.tensor_scalar(smallf[:, 0:1], rV[:, 0:1], 1.0 / E,
                                    None, ALU.mult)
                    v.tensor_reduce(rV[:, 1:2], stgBF[:, 0:E], AX.X, ALU.add)
                    v.tensor_scalar(rV[:, 1:2], rV[:, 1:2], 1.0 / E, None,
                                    ALU.mult)
                    v.tensor_tensor(rV[:, 2:3], smallf[:, 0:1],
                                    smallf[:, 0:1], ALU.mult)
                    v.tensor_tensor(rV[:, 1:2], rV[:, 1:2], rV[:, 2:3],
                                    ALU.subtract)
                    return v.tensor_scalar(rV[:, 0:1], rV[:, 1:2], EPS_LN,
                                           None, ALU.add)
                K.task("vector", v_rows)
                K.task("scalar", lambda s, c=c: s.activation(
                    rV[:, 0:1], rV[:, 0:1], AF.Sqrt))

                def v_ap(v, c=c):
                    v.reciprocal(rV2[:, 0:1], rV[:, 0:1])
                    v.tensor_tensor(smallf[:, 3:4], smallf[:, 0:1],
                                    rV2[:, 0:1], ALU.mult)
                    v.tensor_scalar(smallf[:, 3:4], smallf[:, 3:4], -1.0, None,
                                    ALU.mult)
                    for n in range(2):
                        v.tensor_scalar(junkb[:, 512 * n:512 * (n + 1)],
                                        pBig[:, 512 * n:512 * (n + 1)],
                                        rV2[:, 0:1], smallf[:, 3:4],
                                        ALU.mult, ALU.add)
                    v.tensor_tensor(junkb[:, :], junkb[:, :], c_gvB[:, :],
                                    ALU.mult)
                    return v.tensor_tensor(stgB[:, c % 2, :], junkb[:, :],
                                           c_bvB[:, :], ALU.add)
                K.task("vector", v_ap)
                K.dmas("sync", "dma", [(sp_vn.ap()[c], stgB[:, c % 2, :])])

        _dump("tvk0", sp_tvk.ap()[0], "gpsimd")
        _dump("vn0", sp_vn.ap()[0], "gpsimd")
        if stop <= 7:
            _dummy_fin()
            return nc
        K.dmas("gpsimd", "dmag",
               [(qg_fm[:, :], ag3_out.ap().rearrange("(p n) -> p n", p=128))],
               waits=[t_ag3])

        def qg_build(v):
            v.tensor_scalar(qg_fm[:, :], qg_fm[:, :], 1.0 / (L * 8.0), None,
                            ALU.mult)
            v.memset(qgp_pad[:, :], 0.0)
            v.memset(qgb_pad[:, :], 0.0)
            v.tensor_tensor(smallf[:, 0:FC], qg_fm[:, :], v_fm["gk_fm"][:, :],
                            ALU.mult)
            for j in range(FC):
                for h2 in range(2):
                    v.tensor_copy(qgp_pad[64 * h2:64 * (h2 + 1),
                                          18 * j + h2:18 * j + h2 + 1],
                                  smallf[64 * h2:64 * (h2 + 1), j:j + 1])
            v.tensor_tensor(smallf[:, 0:FC], qg_fm[:, :], v_fm["Bk_fm"][:, :],
                            ALU.mult)
            for j in range(FC):
                for h2 in range(2):
                    v.tensor_copy(qgb_pad[64 * h2:64 * (h2 + 1),
                                          18 * j + h2:18 * j + h2 + 1],
                                  smallf[64 * h2:64 * (h2 + 1), j:j + 1])
            return None
        K.task("vector", qg_build)
        _dump("qg", qg_fm[:, :])

        def co_mm(t):
            for j in range(FC):
                t.matmul(pA[0:16, 0:1], qgb_pad[:, 16 * j:16 * (j + 1)],
                         onescol[:, :], start=(j == 0), stop=(j == FC - 1))
            for j in range(FC):
                t.matmul(pA[0:16, 1:2], qgp_pad[:, 16 * j:16 * (j + 1)],
                         onescol[:, :], start=(j == 0), stop=(j == FC - 1))
            return None
        K.task("tensor", co_mm)

        def co_ev(s):
            s.copy(co16[:, :], pA[0:16, 0:1])
            return s.copy(qgs16[:, :], pA[0:16, 1:2])
        K.task("scalar", co_ev)
        K.dmas("gpsimd", "dmag",
               [(r_qgs[:, :].rearrange("a (c p) -> a p c", p=16), qgs16[:, :])],
               slow=True)

        ykst = WFLAT.rearrange("p (a n) -> p a n", a=4)
        for j in range(FC):
            K.dmas("sync", "dma", [(ykst[:, j % 2, :], sp_yk.ap()[j])])

            def sc_mm(t, j=j):
                for n in range(4):
                    t.matmul(pBig[0:16, 512 * n:512 * (n + 1)],
                             qgp_pad[:, 16 * j:16 * (j + 1)],
                             ykst[:, j % 2, 512 * n:512 * (n + 1)],
                             start=(j == 0), stop=(j == FC - 1))
                return None
            K.task("tensor", sc_mm)

        def sc_r1(t):
            for n in range(4):
                t.matmul([pA, pB, pC, pD][n][0:16, :], r_qgs[0:1, :],
                         r_mkrk[0:1, 512 * n:512 * (n + 1)], start=True,
                         stop=True)
            return None
        K.task("tensor", sc_r1)
        K.task("scalar", lambda s: s.copy(rowC[0:16, 0:TOK],
                                          pBig[0:16, 0:TOK]))

        def sc_rskB(t):
            for n in range(4):
                t.matmul(pBig[0:16, 512 * n:512 * (n + 1)], ones16[0:1, :],
                         r_rsY[0:1, 512 * n:512 * (n + 1)], start=True,
                         stop=True)
            return None
        K.task("tensor", sc_rskB)

        def sc_fin(v):
            for n in range(4):
                v.tensor_tensor(rowC[0:16, 512 * n:512 * (n + 1)],
                                rowC[0:16, 512 * n:512 * (n + 1)],
                                pBig[0:16, 512 * n:512 * (n + 1)], ALU.mult)
            for n in range(4):
                v.scalar_tensor_tensor(rowC[0:16, 512 * n:512 * (n + 1)],
                                       rowC[0:16, 512 * n:512 * (n + 1)], 1.0,
                                       [pA, pB, pC, pD][n][0:16, :],
                                       ALU.mult, ALU.subtract)
            v.tensor_scalar(rowC[0:16, 0:TOK], rowC[0:16, 0:TOK], co16[:, 0:1],
                            None, ALU.add)
            v.tensor_reduce(sc_Mi[:, :], rowC[0:16, 0:TOK], AX.X, ALU.max)
            v.tensor_tensor(sc_Mi[:, :], sc_Mi[:, :], c_zpad[:, :], ALU.max)
            return v.tensor_scalar(sc_negM[:, :], sc_Mi[:, :], -1.0, None,
                                   ALU.mult)
        K.task("vector", sc_fin)
        K.task("scalar", lambda s: s.activation(
            rowD[0:16, 0:TOK], rowC[0:16, 0:TOK], AF.Exp,
            bias=sc_negM[:, 0:1]))
        K.task("vector", lambda v: v.tensor_reduce(
            sc_Zi[:, :], rowD[0:16, 0:TOK], AX.X, ALU.add))
        K.task("vector", lambda v: v.tensor_tensor(
            sc_t16[:, 0:1], c_zpad[:, :], sc_Mi[:, :], ALU.subtract))
        K.task("scalar", lambda s: s.activation(sc_t16[:, 1:2], sc_t16[:, 0:1],
                                                AF.Exp))
        K.task("vector", lambda v: v.tensor_tensor(sc_Zi[:, :], sc_Zi[:, :],
                                                   sc_t16[:, 1:2], ALU.add))
        if "sc" in DBG:
            K.dmas("sync", "dma", [(dbg_t["sc"].ap(), rowC[0:16, 0:TOK])])
        if "MZ" in DBG:
            K.dmas("sync", "dma", [(dbg_t["MZ"].ap()[:, 0:1], sc_Mi[:, :]),
                                   (dbg_t["MZ"].ap()[:, 1:2], sc_Zi[:, :])],
                   slow=True)
        K.dmas("gpsimd", "dmag", [(ag4_in.ap()[0:16], sc_Mi[:, 0:1]),
                                  (ag4_in.ap()[16:32], sc_Zi[:, 0:1])])

        def ag4_go(g):
            return g.collective_compute(
                "AllGather", ALU.bypass,
                replica_groups=[[0, 1, 2, 3], [4, 5, 6, 7]],
                ins=[ag4_in.ap().opt()], outs=[ag4_out.ap().opt()])
        K.task("gpsimd", ag4_go)
        K.dmas("gpsimd", "dmag",
               [(sc_Mall[:, r:r + 1], ag4_out.ap()[r, 0:16])
                for r in range(4)] +
               [(sc_Zall[:, r:r + 1], ag4_out.ap()[r, 16:32])
                for r in range(4)])

        def comb1(v):
            v.tensor_reduce(sc_Mc[:, :], sc_Mall[:, :], AX.X, ALU.max)
            return v.tensor_scalar(sc_t16[:, 0:4], sc_Mall[:, :],
                                   sc_Mc[:, 0:1], None, ALU.subtract)
        K.task("vector", comb1)
        K.task("scalar", lambda s: s.activation(sc_t16[:, 0:4],
                                                sc_t16[:, 0:4], AF.Exp))

        def comb2(v):
            v.tensor_tensor(sc_t16[:, 0:4], sc_t16[:, 0:4],
                            sc_Zall[:, :], ALU.mult)
            v.tensor_reduce(sc_Zc[:, :], sc_t16[:, 0:4], AX.X, ALU.add)
            v.tensor_tensor(sc_t16[:, 0:1], sc_Mi[:, :], sc_Mc[:, :],
                            ALU.subtract)
            return v.tensor_scalar(sc_t16[:, 1:2], sc_Mc[:, :], -1.0, None,
                                   ALU.mult)
        K.task("vector", comb2)

        def comb3(s):
            s.activation(sc_t16[:, 0:1], sc_t16[:, 0:1], AF.Exp)
            return s.activation(sc_t16[:, 1:2], sc_t16[:, 1:2], AF.Exp)
        K.task("scalar", comb3)

        def alpha_v(v):
            v.reciprocal(sc_fac[:, :], sc_Zc[:, :])
            v.tensor_scalar(sc_fac[:, :], sc_fac[:, :], float(L), None, ALU.mult)
            v.tensor_tensor(sc_a0[:, :], sc_t16[:, 1:2], sc_fac[:, :], ALU.mult)
            v.tensor_tensor(sc_fac[:, :], sc_fac[:, :], sc_t16[:, 0:1], ALU.mult)
            v.tensor_scalar(hm16[:, 0:TOK], rowD[0:16, 0:TOK],
                            sc_fac[:, 0:1], None,
                            ALU.mult)
            return v.tensor_copy(a0_bf[:, :], sc_a0[:, :])
        K.task("vector", alpha_v)
        if "alpha" in DBG:
            K.dmas("gpsimd", "dmag", [(dbg_t["alpha"].ap(),
                                       hm16[0:16, 0:TOK])])

        if stop <= 8:
            _dummy_fin()
            return nc
        with nc.sbuf_tensor("pksb", [128, 2, E], BF16) as pksb, \
             nc.sbuf_tensor("kvacc", [64, 1024], F32) as kvacc, \
             nc.sbuf_tensor("tvkst", [128, 2, E], BF16) as tvkst, \
             nc.sbuf_tensor("vnst", [128, 2, E], BF16) as vnst:
            for c in range(TC):
                K.dmas("sync", "dma", [(tvkst[:, c % 2, :], sp_tvk.ap()[c]),
                                       (vnst[:, c % 2, :], sp_vn.ap()[c])])

                def aE(t, c=c):
                    for n in range(2):
                        t.matmul(pBig[:, 512 * n:512 * (n + 1)],
                                 hm16[:, 128 * c:128 * (c + 1)],
                                 R16[:, 512 * n:512 * (n + 1)],
                                 start=True, stop=True)
                    return None
                K.task("tensor", aE)

                def pks_v(v, c=c):
                    for n in range(2):
                        v.scalar_tensor_tensor(
                            pksb[:, c % 2, 512 * n:512 * (n + 1)],
                            tvkst[:, c % 2, 512 * n:512 * (n + 1)], 1.0,
                            pBig[:, 512 * n:512 * (n + 1)], ALU.add, ALU.mult)
                    return None
                K.task("vector", pks_v)
                if c == 0 and "aE0" in DBG:
                    K.task("scalar", lambda s:
                           (s.copy(junk[:, 0:512], pBig[:, 0:512]),
                            s.copy(junk[:, 512:1024], pBig[:, 512:1024]))[-1])
                    K.dmas("sync", "dma",
                           [(dbg_t["aE0"].ap(), junk[:, 0:1024])])
                if c == 0 and "pks0" in DBG:
                    K.dmas("gpsimd", "dmag",
                           [(dbg_t["pks0"].ap(), pksb[:, 0, :])])

                def kv_mm(t, c=c):
                    for h in range(Hh):
                        t.matmul(pBig[0:64, 1024 + 64 * h:1024 + 64 * (h + 1)],
                                 pksb[:, c % 2, 64 * h:64 * (h + 1)],
                                 vnst[:, c % 2, 64 * h:64 * (h + 1)],
                                 start=True, stop=True)
                    t.matmul(pD[0:1, :], onescol[:, :], pksb[:, c % 2, 0:512],
                             start=(c == 0), stop=(c == TC - 1))
                    t.matmul(pC[0:1, :], onescol[:, :], pksb[:, c % 2, 512:1024],
                             start=(c == 0), stop=(c == TC - 1))
                    return None
                K.task("tensor", kv_mm)

                def kv_acc(v, c=c):
                    if c == 0:
                        v.tensor_copy(kvacc[:, 0:512], pBig[0:64, 1024:1536])
                        v.tensor_copy(kvacc[:, 512:1024],
                                      pBig[0:64, 1536:2048])
                    else:
                        v.tensor_tensor(kvacc[:, 0:512], kvacc[:, 0:512],
                                        pBig[0:64, 1024:1536], ALU.add)
                        v.tensor_tensor(kvacc[:, 512:1024],
                                        kvacc[:, 512:1024],
                                        pBig[0:64, 1536:2048], ALU.add)
                    return None
                K.task("vector", kv_acc)

            def ct_ev(s):
                s.copy(junk[0:64, 0:1024], kvacc[:, :])
                s.copy(lrow[0:1, 0:512], pD[0:1, :])
                return s.copy(lrow[0:1, 512:1024], pC[0:1, :])
            K.task("scalar", ct_ev)
            K.dmas("sync", "dma", [
                (ag5_in.ap()[0:64 * 1024].rearrange("(p n) -> p n", p=64),
                 junk[0:64, 0:1024]),
                (ag5_in.ap()[64 * 1024:65 * 1024].rearrange("(a n) -> a n",
                                                            a=1),
                 lrow[0:1, 0:1024]),
            ])

        def ag5_go(g):
            return g.collective_compute(
                "AllReduce", ALU.add,
                replica_groups=[[0, 1, 2, 3], [4, 5, 6, 7]],
                ins=[ag5_in.ap().opt()], outs=[ag5_out.ap().opt()])
        t_ag5 = K.task("gpsimd", ag5_go)

        if stop <= 9:
            _dummy_fin()
            return nc
        with nc.sbuf_tensor("gu_t", [128, 2, INNER], F32) as gu_t, \
             nc.sbuf_tensor("terng", [128, 1, 1024], BF16) as terng, \
             nc.sbuf_tensor("ternu", [128, 1, 1024], BF16) as ternu, \
             nc.sbuf_tensor("xq4", [128, 2 * FC, 128], BF16) as xq4, \
             nc.sbuf_tensor("guq_tm", [128, INNER], BF16) as guq_tm, \
             nc.sbuf_tensor("guq_fmt", [128, IC, 128], BF16,
                            is_transpose_dest=True) as guq_fmt, \
             nc.sbuf_tensor("ffs", [128, 8], F32) as ffs:
            NP4 = 1 if (90 <= stop < 99) else 8
            for p4 in range(NP4):
                tchunks = [2 * p4, 2 * p4 + 1]
                K.dmas("sync", "dma",
                       [(xq4[:, i * FC + k, :],
                         sp_xq.ap()[k, :, 128 * tc:128 * (tc + 1)])
                        for i, tc in enumerate(tchunks) for k in range(FC)],
                       waits=[t_ag2])

                def rsc(v, tchunks=tchunks):
                    for i, tc in enumerate(tchunks):
                        v.tensor_scalar(ffs[:, i:i + 1], st_r[:, tc:tc + 1],
                                        p_sw[:, 0:1], None, ALU.mult)
                        v.tensor_scalar(ffs[:, 2 + i:3 + i], st_r[:, tc:tc + 1],
                                        p_sw[:, 1:2], None, ALU.mult)
                    return None
                K.task("vector", rsc)
                if stop == 91:
                    _dummy_fin()
                    return nc

                NE4 = 1 if (90 <= stop < 99) else 4
                for e in range(NE4):
                    for k in range(FC):
                        K.dmas("sync", "dma", [
                            (terng[:, 0, :],
                             ag2v(k, "wg")[:, 1024 * e:1024 * (e + 1)]),
                            (ternu[:, 0, :],
                             ag2v(k, "wu")[:, 1024 * e:1024 * (e + 1)]),
                        ])

                        def gu_mm(t, e=e, k=k):
                            for i in range(2):
                                for n in range(2):
                                    t.matmul(pBig[:, 1024 * i + 512 * n:
                                                  1024 * i + 512 * (n + 1)],
                                             xq4[:, i * FC + k, :],
                                             terng[:, 0,
                                                   512 * n:512 * (n + 1)],
                                             start=(k == 0),
                                             stop=(k == FC - 1))
                                    pp = [pA, pB, pC, pD][2 * i + n]
                                    t.matmul(pp[:, :], xq4[:, i * FC + k, :],
                                             ternu[:, 0,
                                                   512 * n:512 * (n + 1)],
                                             start=(k == 0),
                                             stop=(k == FC - 1))
                            return None
                        K.task("tensor", gu_mm)
                    if stop == 92:
                        _dummy_fin()
                        return nc

                    for i in range(2):
                        K.task("scalar", lambda s, i=i: s.activation(
                            junk[:, 0:1024], pBig[:, 1024 * i:1024 * (i + 1)],
                            AF.Silu, scale=ffs[:, i:i + 1]))

                        def gu_v(v, i=i, e=e):
                            for n in range(2):
                                v.scalar_tensor_tensor(
                                    gu_t[:, i, 1024 * e + 512 * n:
                                         1024 * e + 512 * (n + 1)],
                                    [pA, pB, pC, pD][2 * i + n][:, :],
                                    ffs[:, 2 + i:3 + i],
                                    junk[:, 512 * n:512 * (n + 1)],
                                    ALU.mult, ALU.mult)
                            return None
                        K.task("vector", gu_v)
                    if stop == 93:
                        _dummy_fin()
                        return nc

                for i, tc in enumerate(tchunks):
                    def quant(v, i=i):
                        v.tensor_reduce(rV[:, 0:1], gu_t[:, i, :], AX.X,
                                        ALU.max, apply_absolute_value=True)
                        v.tensor_scalar(rV[:, 1:2], rV[:, 0:1], 1e-5,
                                        1.0 / 127.0, ALU.max, ALU.mult)
                        v.tensor_copy(rV[:, 0:1], rV[:, 1:2])
                        v.reciprocal(rV2[:, 0:1], rV[:, 0:1])
                        v.tensor_scalar(rV[:, 1:2], rV[:, 1:2],
                                        p_sw[:, 2:3], None, ALU.mult)
                        v.tensor_scalar(gu_t[:, i, :], gu_t[:, i, :],
                                        rV2[:, 0:1], CROUND, ALU.mult, ALU.add)
                        return v.tensor_scalar(guq_tm[:, :], gu_t[:, i, :],
                                               -CROUND, rV[:, 1:2], ALU.add,
                                               ALU.mult)
                    K.task("vector", quant)
                    if stop == 94:
                        _dummy_fin()
                        return nc
                    K.task("scalar", lambda s:
                           (s.dma_start_transpose(guq_fmt[:, :, :],
                                                  guq_tm[:, :])
                            .then_inc(K.sem["dmas"], 16), 1), dma="dmas")
                    if stop == 95:
                        _dummy_fin()
                        return nc
                    K.dmas("sync", "dma", [(sp_guq.ap()[tc], guq_fmt[:, :, :])])
                    if stop == 96:
                        _dummy_fin()
                        return nc

        _dump("guq00", sp_guq.ap()[0][:, 0, :], "gpsimd")
        _dump("psw", p_sw[:, :])
        if stop <= 10:
            _dummy_fin()
            return nc
        with nc.sbuf_tensor("attnA", [128, FC, HTOK], BF16) as attnA:
            with nc.sbuf_tensor("tvqr", [128, FC, HTOK], BF16) as tvqr, \
                 nc.sbuf_tensor("pqr", [128, 1032], BF16) as pqr, \
                 nc.sbuf_tensor("ctac", [65, 1024], F32) as ctac:

                K.dmas("gpsimd", "dmag",
                       [(ctac[:, :],
                         ag5_out.ap().rearrange("(p n) -> p n", p=65))],
                       waits=[t_ag5])

                _dump("ct", ctac[:, :])
                K.task("scalar", lambda s: s.activation(
                    pk0_fm[:, :], v_fm["bphik_fm"][:, :], AF.Tanh))

                def a0_mm(t):
                    for j in range(FC):
                        t.matmul(pA[:, j:j + 1], R16[:, 128 * j:128 * (j + 1)],
                                 a0_bf[:, :], start=True, stop=True)
                    return None
                K.task("tensor", a0_mm)
                K.dmas("sync", "dma",
                       [(ksum_fm[:, c:c + 1],
                         ctac[64:65, 128 * c:128 * (c + 1)])
                        for c in range(FC)], slow=True)

                def ks_build(v):
                    v.tensor_scalar(pk0_fm[:, :], pk0_fm[:, :], 1.0, None,
                                    ALU.add)
                    v.tensor_tensor(smallf[:, 0:FC], pk0_fm[:, :], pA[:, 0:FC],
                                    ALU.mult)
                    v.tensor_tensor(ksum_fm[:, :], ksum_fm[:, :],
                                    smallf[:, 0:FC], ALU.add)
                    v.memset(ks_pad[:, :], 0.0)
                    for j in range(FC):
                        for h2 in range(2):
                            v.tensor_copy(ks_pad[64 * h2:64 * (h2 + 1),
                                                 18 * j + h2:18 * j + h2 + 1],
                                          ksum_fm[64 * h2:64 * (h2 + 1),
                                                  j:j + 1])
                    v.memset(kvbd[:, :, :], 0.0)
                    for j in range(FC):
                        v.tensor_copy(kvbd[0:64, j, 0:64],
                                      ctac[0:64, 128 * j:128 * j + 64])
                    return None
                K.task("vector", ks_build)
                _dump("ksum", ksum_fm[:, :])
                K.dmas("gpsimd", "dmag",
                       [(kvbd[64:128, j, 64:128],
                         ctac[0:64, 128 * j + 64:128 * (j + 1)])
                        for j in range(FC)])

                def hs_mm(t):
                    for j in range(FC):
                        t.matmul(pA[0:16, 8:9], ks_pad[:, 16 * j:16 * (j + 1)],
                                 onescol[:, :], start=(j == 0),
                                 stop=(j == FC - 1))
                    return None
                K.task("tensor", hs_mm)
                K.task("scalar", lambda s: s.copy(hs16[:, :], pA[0:16, 8:9]))
                K.dmas("gpsimd", "dmag",
                       [(r_hsum[:, :].rearrange("a (c p) -> a p c", p=16),
                         hs16[:, :])], slow=True)
                K.dmas("sync", "dma", [(tvqr[:, j, :], sp_tvq.ap()[j])
                                       for j in range(FC)])

                NSL = [(0, 512), (512, 512), (1024, 512), (1536, 512)]

                def den_mm(t):
                    for j in range(FC):
                        for off, ln in NSL:
                            t.matmul(pBig[0:16, off:off + ln],
                                     ks_pad[:, 16 * j:16 * (j + 1)],
                                     tvqr[:, j, off:off + ln],
                                     start=(j == 0), stop=False)
                        t.matmul(pA[0:16, 12:16],
                                 ks_pad[:, 16 * j:16 * (j + 1)],
                                 tvqr[:, j, 2048:2052],
                                 start=(j == 0), stop=False)
                    for off, ln in NSL:
                        t.matmul(pBig[0:16, off:off + ln], r_hsum[0:1, :],
                                 onesrow[0:1, off:off + ln], start=False,
                                 stop=True)
                    return t.matmul(pA[0:16, 12:16], r_hsum[0:1, :],
                                    onesrow[0:1, 2048:2052], start=False,
                                    stop=True)
                K.task("tensor", den_mm)

                def den_v(v):
                    for n in range(4):
                        v.tensor_scalar(rowC[0:16, 512 * n:512 * (n + 1)],
                                        pBig[0:16, 512 * n:512 * (n + 1)],
                                        EPS_DEN, None, ALU.add)
                    v.tensor_scalar(rowC[0:16, TOK:HTOK], pA[0:16, 12:16],
                                    EPS_DEN, None, ALU.add)
                    v.reciprocal(rowC[0:16, 0:HTOK], rowC[0:16, 0:HTOK])
                    return v.tensor_copy(hm16[:, 0:HTOK], rowC[0:16, 0:HTOK])
                K.task("vector", den_v)
                if "rden" in DBG:
                    K.dmas("sync", "dma", [(dbg_t["rden"].ap(),
                                            rowC[0:16, 0:HTOK])])

                for j in range(FC):
                    for hh in range(2):
                        off = hh * 1026

                        def rdB(t, j=j, off=off):
                            for n0 in range(0, 1026, 512):
                                nn = min(512, 1026 - n0)
                                t.matmul(pBig[:, n0:n0 + nn],
                                         R16[:, 128 * j:128 * (j + 1)],
                                         hm16[:, off + n0:off + n0 + nn],
                                         start=True, stop=True)
                            return None
                        K.task("tensor", rdB)

                        def pqr_v(v, j=j, off=off):
                            for n0, nn in [(0, 512), (512, 512), (1024, 2)]:
                                v.scalar_tensor_tensor(
                                    pqr[:, n0:n0 + nn],
                                    tvqr[:, j, off + n0:off + n0 + nn], 1.0,
                                    pBig[:, n0:n0 + nn], ALU.add, ALU.mult)
                            return None
                        K.task("vector", pqr_v)

                        def num_mm(t, j=j):
                            for n0, pp in [(0, pA), (512, pB)]:
                                t.matmul(pp[:, 0:512], kvbd[:, j, :],
                                         pqr[:, n0:n0 + 512], start=True,
                                         stop=True)
                            return t.matmul(pC[:, 0:2], kvbd[:, j, :],
                                            pqr[:, 1024:1026], start=True,
                                            stop=True)
                        K.task("tensor", num_mm)

                        def attn_ev(s, j=j, off=off):
                            s.copy(attnA[:, j, off:off + 512], pA[:, 0:512])
                            s.copy(attnA[:, j, off + 512:off + 1024],
                                   pB[:, 0:512])
                            return s.copy(attnA[:, j, off + 1024:off + 1026],
                                          pC[:, 0:2])
                        K.task("scalar", attn_ev)

            if stop <= 11:
                _dummy_fin()
                return nc
            with nc.sbuf_tensor("attnB", [128, FC, TOK], BF16) as attnB:
                ABF = attnB.ap().rearrange("p a n -> p (a n)")

                def mask_v(v):
                    for j in range(FC):
                        v.tensor_tensor(attnA[:, j, 0:2], attnA[:, j, 0:2],
                                        c_cmask[:, 0:2], ALU.mult)
                        v.tensor_tensor(attnA[:, j, TOK + 2:TOK + 4],
                                        attnA[:, j, TOK + 2:TOK + 4],
                                        c_cmask[:, 2:4], ALU.mult)
                    return None
                K.task("vector", mask_v)
                _dump("attn0", attnA[:, 0, :], "gpsimd")

                K.dmas("gpsimd", "dmag",
                       [(WFLAT[:, 128 * (j * 5 + k5):128 * (j * 5 + k5 + 1)],
                         convw_bd.ap()[j, k5])
                        for j in range(FC) for k5 in range(5)])
                K.task("vector", lambda v: v.tensor_scalar(
                    gcb_fm[:, :], v_fm["convb_fm"][:, :], c_gammaB[:, 0:1],
                    None, ALU.mult))

                for j in range(FC):
                    for half in range(2):
                        o0 = half * 1024

                        def conv_mm(t, j=j, o0=o0):
                            for k5 in range(5):
                                idx = j * 5 + k5
                                for n in range(2):
                                    t.matmul(
                                        pBig[:, 512 * n:512 * (n + 1)],
                                        WFLAT[:, 128 * idx:128 * (idx + 1)],
                                        attnA[:, j, k5 + o0 + 512 * n:
                                              k5 + o0 + 512 * (n + 1)],
                                        start=(k5 == 0), stop=(k5 == 4))
                            return None
                        K.task("tensor", conv_mm)
                        K.task("scalar", lambda s, j=j: s.activation(
                            stgC[:, 0, :], pBig[:, 0:1024], AF.Identity,
                            bias=gcb_fm[:, j:j + 1], scale=c_gammaB[:, 0:1]))
                        K.task("vector", lambda v, j=j, o0=o0: v.tensor_tensor(
                            attnB[:, j, o0:o0 + 1024], stgC[:, 0, :],
                            attnA[:, j, 2 + o0:2 + o0 + 1024], ALU.add))

                _dump("attn20", attnB[:, 0, :], "gpsimd")
                for j in range(FC):
                    K.task("scalar", lambda s, j=j: s.activation(
                        attnA[:, j, 0:TOK], attnB[:, j, :], AF.Square))

                def lna_mm(t):
                    for j in range(FC):
                        for n in range(4):
                            t.matmul(pBig[0:1, 512 * n:512 * (n + 1)],
                                     onescol[:, :],
                                     attnB[:, j, 512 * n:512 * (n + 1)],
                                     start=(j == 0), stop=(j == FC - 1))
                    for n in range(4):
                        pp = [pA, pB, pC, pD][n]
                        for j in range(FC):
                            t.matmul(pp[0:1, :], onescol[:, :],
                                     attnA[:, j, 512 * n:512 * (n + 1)],
                                     start=(j == 0), stop=(j == FC - 1))
                    return None
                K.task("tensor", lna_mm)

                def lna_rows(v):
                    for n in range(4):
                        v.tensor_scalar(rowC[0:1, 512 * n:512 * (n + 1)],
                                        pBig[0:1, 512 * n:512 * (n + 1)],
                                        1.0 / E, None, ALU.mult)
                    for n in range(4):
                        v.tensor_scalar(lrow2[0:1, 512 * n:512 * (n + 1)],
                                        [pA, pB, pC, pD][n][0:1, :], 1.0 / E,
                                        None, ALU.mult)
                    v.tensor_tensor(lrow[0:1, 0:TOK], rowC[0:1, 0:TOK],
                                    rowC[0:1, 0:TOK], ALU.mult)
                    v.tensor_tensor(lrow2[0:1, 0:TOK], lrow2[0:1, 0:TOK],
                                    lrow[0:1, 0:TOK], ALU.subtract)
                    return v.tensor_scalar(lrow2[0:1, 0:TOK], lrow2[0:1, 0:TOK],
                                           EPS_LN, None, ALU.add)
                K.task("vector", lna_rows)
                K.task("scalar", lambda s: s.activation(
                    lrow2[0:1, 0:TOK], lrow2[0:1, 0:TOK], AF.Sqrt))

                def lna_rows2(v):
                    v.reciprocal(lrow[0:1, 0:TOK], lrow2[0:1, 0:TOK])
                    v.tensor_copy(r_rsY[0:1, 0:TOK], lrow[0:1, 0:TOK])
                    v.tensor_tensor(lrow[0:1, 0:TOK], rowC[0:1, 0:TOK],
                                    lrow[0:1, 0:TOK], ALU.mult)
                    v.tensor_scalar(lrow[0:1, 0:TOK], lrow[0:1, 0:TOK], -1.0,
                                    None, ALU.mult)
                    return v.tensor_copy(r_mkrk[0:1, 0:TOK], lrow[0:1, 0:TOK])
                K.task("vector", lna_rows2)

                for half in range(2):
                    o0 = half * 1024

                    def lap_b(t, o0=o0):
                        for n in range(2):
                            t.matmul(pBig[:, 512 * n:512 * (n + 1)],
                                     onesrow[0:1, 0:128],
                                     r_rsY[0:1,
                                           o0 + 512 * n:o0 + 512 * (n + 1)],
                                     start=True, stop=True)
                            t.matmul(pBig[:, 1024 + 512 * n:
                                          1024 + 512 * (n + 1)],
                                     onesrow[0:1, 0:128],
                                     r_mkrk[0:1,
                                            o0 + 512 * n:o0 + 512 * (n + 1)],
                                     start=True, stop=True)
                        return None
                    K.task("tensor", lap_b)

                    for j in range(FC):
                        def lap_v(v, j=j, o0=o0):
                            for n in range(2):
                                v.tensor_tensor(
                                    junk[:, 512 * n:512 * (n + 1)],
                                    attnB[:, j, o0 + 512 * n:
                                          o0 + 512 * (n + 1)],
                                    pBig[:, 512 * n:512 * (n + 1)], ALU.mult)
                                v.tensor_tensor(
                                    attnA[:, j, o0 + 512 * n:
                                          o0 + 512 * (n + 1)],
                                    junk[:, 512 * n:512 * (n + 1)],
                                    pBig[:, 1024 + 512 * n:
                                         1024 + 512 * (n + 1)], ALU.add)
                            return None
                        K.task("vector", lap_v)

                _dump("ln0", attnA[:, 0, 0:TOK], "gpsimd")

                def boga_b(v):
                    v.reciprocal(smallf[:, 0:FC], v_fm["ga_fm"][:, :])
                    v.tensor_tensor(smallf[:, 0:FC], v_fm["Ba_fm"][:, :],
                                    smallf[:, 0:FC], ALU.mult)
                    return v.tensor_copy(bog[:, :], smallf[:, 0:FC])
                K.task("vector", boga_b)

                w1st = WFLAT.rearrange("p (a n) -> p a n", a=2)
                for half in range(2):
                    for k in range(FC):
                        K.dmas("sync", "dma",
                               [(w1st[:, k % 2, :], ag2v(k, "w1"))],
                               waits=[t_ag2])

                        def b1_mm(t, k=k, half=half):
                            for n in range(4):
                                t.matmul(pBig[0:1, 512 * n:512 * (n + 1)],
                                         bog[:, k:k + 1],
                                         w1st[:, k % 2,
                                              half * 2048 + 512 * n:
                                              half * 2048 + 512 * (n + 1)],
                                         start=(k == 0), stop=(k == FC - 1))
                            return None
                        K.task("tensor", b1_mm)
                    K.task("scalar", lambda s: s.copy(lrow[0:1, 0:TOK],
                                                      pBig[0:1, 0:TOK]))
                    K.dmas("sync", "dma",
                           [(b1r_fm[:, 16 * half + c:16 * half + c + 1],
                             lrow[0:1, 128 * c:128 * (c + 1)])
                            for c in range(16)], slow=True)

                K.dmas("gpsimd", "dmag",
                       [(rowtmp2[0:1, :], rows_in["b2_row"].ap())])
                K.dmas("sync", "dma",
                       [(lrow[0:1, 0:E].rearrange("a n -> a n"),
                         rows_in["Ba_row"].ap())])

                def b1p_v(v):
                    v.tensor_tensor(v_b1p[:, :], v_b1[:, :], b1r_fm[:, :],
                                    ALU.add)
                    v.tensor_tensor(rowtmp2[0:1, :], rowtmp2[0:1, :],
                                    lrow[0:1, 0:E], ALU.add)
                    v.tensor_copy(b2pBa_bf[:, :], rowtmp2[0:1, :])
                    v.tensor_copy(junkb[:, :], c_gaB32[:, :])
                    for j in range(FC):
                        v.tensor_tensor(c_gaI[:, 128 * j:128 * (j + 1)],
                                        ident[:, :],
                                        junkb[:, 128 * j:128 * (j + 1)],
                                        ALU.mult)
                    return None
                K.task("vector", b1p_v)

                if stop <= 12:
                    _dummy_fin()
                    return nc
                for q4 in range(4):
                    t0 = 512 * q4
                    tchunks = [q4 * 4 + i for i in range(4)]
                    for o in range(IC):
                        K.dmas("sync", "dma",
                               [(stgA[:, o % 2, 128 * k:128 * (k + 1)],
                                 ag2v(k, "w1")[:, 128 * o:128 * (o + 1)])
                                for k in range(FC)])

                        def w1_mm(t, o=o, t0=t0):
                            pp = [pA, pB][o % 2]
                            for k in range(FC):
                                t.matmul(pp[:, :],
                                         stgA[:, o % 2, 128 * k:128 * (k + 1)],
                                         attnA[:, k, t0:t0 + 512],
                                         start=(k == 0), stop=(k == FC - 1))
                            return None
                        K.task("tensor", w1_mm)
                        K.task("scalar", lambda s, o=o: s.activation(
                            ABF[:, 512 * o:512 * (o + 1)],
                            [pA, pB][o % 2][:, :], AF.Gelu,
                            bias=v_b1p[:, o:o + 1]))

                    for i, tc in enumerate(tchunks):
                        K.dmas("sync", "dma",
                               [(stgC[:, 0, :],
                                 sp_outacc.ap()[128 * tc:128 * (tc + 1), :])])
                        for o in range(IC):
                            r, roff = o // 4, (o % 4) * 128
                            K.dmas("sync", "dma", [
                                (stgB[:, o % 2, 0:E],
                                 ag2v(r, "w2")[roff:roff + 128, :]),
                                (WST[:, o % 2, 0:E],
                                 ag2v(r, "wd")[roff:roff + 128, :]),
                            ])
                            K.dmas("sync", "dma",
                                   [(WST[:, o % 2, E:E + 128],
                                     sp_guq.ap()[tc][:, o, :])])

                            def fin_mm(t, o=o, i=i):
                                for n in range(2):
                                    t.matmul(pBig[:, 512 * n:512 * (n + 1)],
                                             ABF[:, 512 * o + 128 * i:
                                                 512 * o + 128 * (i + 1)],
                                             stgB[:, o % 2,
                                                  512 * n:512 * (n + 1)],
                                             start=(o == 0), stop=False)
                                    t.matmul(pBig[:, 512 * n:512 * (n + 1)],
                                             WST[:, o % 2, E:E + 128],
                                             WST[:, o % 2,
                                                 512 * n:512 * (n + 1)],
                                             start=False, stop=False)
                                return None
                            K.task("tensor", fin_mm)

                        def fin_tail(t, tc=tc):
                            for k in range(FC):
                                t.matmul(pBig[:, 128 * k:128 * (k + 1)],
                                         attnA[:, k, 128 * tc:128 * (tc + 1)],
                                         c_gaI[:, 128 * k:128 * (k + 1)],
                                         start=False, stop=False)
                            for n in range(2):
                                t.matmul(pBig[:, 512 * n:512 * (n + 1)],
                                         onesrow[0:1, 0:128],
                                         b2pBa_bf[0:1, 512 * n:512 * (n + 1)],
                                         start=False, stop=True)
                            return None
                        K.task("tensor", fin_tail)
                        def fin_v(v, tc=tc):
                            for n in range(2):
                                v.tensor_tensor(
                                    fout[:, 512 * n:512 * (n + 1)],
                                    pBig[:, 512 * n:512 * (n + 1)],
                                    stgC[:, 0, 512 * n:512 * (n + 1)], ALU.add)
                            return None
                        K.task("vector", fin_v)
                        K.dmas("sync", "dma",
                               [(out.ap()[128 * tc:128 * (tc + 1), :],
                                 fout[:, :])])

        K.task("sync", lambda q: None)

    return nc


def _host_constants():
    import ml_dtypes
    bf = ml_dtypes.bfloat16
    ident = np.eye(128, dtype=bf)
    R16 = np.zeros((16, E), dtype=bf)
    for h in range(Hh):
        R16[h, h * Dd:(h + 1) * Dd] = 1
    onesrow = np.ones((1, HTOK), dtype=bf)
    onescol = np.ones((128, 1), dtype=bf)
    ones16 = np.ones((1, 16), dtype=bf)
    return ident, R16, onesrow, onescol, ones16


def make_in_maps(inputs):
    x = np.asarray(inputs["L0_attn_x"], dtype=np.float32)
    g = lambda n: np.ascontiguousarray(np.asarray(inputs[n], dtype=np.float32))
    Wq, Wk, Wv = g("L0_attn_Wq"), g("L0_attn_Wk"), g("L0_attn_Wv")
    Wpq, Wpk = g("L0_attn_Wphi_q"), g("L0_attn_Wphi_k")
    W1, W2 = g("L0_attn_W1"), g("L0_attn_W2")
    wgT = np.ascontiguousarray(g("L0_ffn_w_gate").T)
    wuT = np.ascontiguousarray(g("L0_ffn_w_up").T)
    wdT = np.ascontiguousarray(g("L0_ffn_w_down").T)
    conv_w = g("L0_attn_conv_w")
    gamma = np.float32(np.asarray(inputs["L0_attn_gamma"], dtype=np.float32))
    vec = {k: g(n) for k, n in [
        ("bq", "L0_attn_bq"), ("bk", "L0_attn_bk"), ("bv", "L0_attn_bv"),
        ("gq", "L0_attn_lnq_g"), ("Bq", "L0_attn_lnq_b"),
        ("gk", "L0_attn_lnk_g"), ("Bk", "L0_attn_lnk_b"),
        ("gv", "L0_attn_lnv_g"), ("Bv", "L0_attn_lnv_b"),
        ("ga", "L0_attn_lna_g"), ("Ba", "L0_attn_lna_b"),
        ("bphiq", "L0_attn_bphi_q"), ("bphik", "L0_attn_bphi_k"),
        ("b1", "L0_attn_b1"), ("b2", "L0_attn_b2"), ("cb", "L0_attn_conv_b")]}
    fm = lambda v: np.ascontiguousarray(v.reshape(FC, 128).T)
    fm32 = lambda v: np.ascontiguousarray(v.reshape(IC, 128).T)
    row = lambda v: np.ascontiguousarray(v.reshape(1, -1).astype(np.float32))
    convbd = np.zeros((FC, 5, 128, 128), np.float32)
    for j in range(FC):
        for k5 in range(5):
            for half in range(2):
                ob = 128 * j + 64 * half
                blkw = conv_w[ob:ob + 64, :, k5]
                convbd[j, k5, 64 * half:64 * half + 64,
                       64 * half:64 * half + 64] = blkw.T
    ident, R16c, onesrow, onescol, ones16 = _host_constants()
    in_maps = []
    for c in range(NCORE):
        b = c // 4
        t0 = TOK * (c % 4)
        xs = np.ascontiguousarray(x[b, t0:t0 + TOK])
        halo = np.zeros((4, E), np.float32)
        if t0 - 2 >= 0:
            halo[0:2] = x[b, t0 - 2:t0]
        if t0 + TOK + 2 <= L:
            halo[2:4] = x[b, t0 + TOK:t0 + TOK + 2]
        cmask = np.ones((128, 4), np.float32)
        if t0 == 0:
            cmask[:, 0:2] = 0
        if t0 + TOK == L:
            cmask[:, 2:4] = 0
        zpad = np.full((16, 1), 0.0 if t0 == 0 else -1e30, np.float32)
        m = {
            "x_tm": xs, "x_fmT": np.ascontiguousarray(xs.T),
            "x_halo": halo, "x_haloT": np.ascontiguousarray(halo.T),
            "Wq": Wq, "Wk": Wk, "Wv": Wv, "Wphi_q": Wpq, "Wphi_k": Wpk,
            "w2_slice": np.ascontiguousarray(W2[512 * c:512 * (c + 1)]),
            "wg_slice": np.ascontiguousarray(wgT[128 * c:128 * (c + 1)]),
            "wu_slice": np.ascontiguousarray(wuT[128 * c:128 * (c + 1)]),
            "wd_slice": np.ascontiguousarray(wdT[512 * c:512 * (c + 1)]),
            "w1_slice": np.ascontiguousarray(W1[128 * c:128 * (c + 1)]),
            "ga_slice": np.ascontiguousarray(
                vec["ga"][128 * c:128 * (c + 1)].reshape(128, 1)),
            "convw_bd": convbd,
            "bq_fm": fm(vec["bq"]), "bk_fm": fm(vec["bk"]),
            "gq_fm": fm(vec["gq"]), "Bq_fm": fm(vec["Bq"]),
            "gk_fm": fm(vec["gk"]), "Bk_fm": fm(vec["Bk"]),
            "ga_fm": fm(vec["ga"]), "Ba_fm": fm(vec["Ba"]),
            "bphik_fm": fm(vec["bphik"]), "convb_fm": fm(vec["cb"]),
            "b1_fm": fm32(vec["b1"]),
            "bq_row": row(vec["bq"]), "bk_row": row(vec["bk"]),
            "bv_row": row(vec["bv"]), "bphiq_row": row(vec["bphiq"]),
            "bphik_row": row(vec["bphik"]), "b2_row": row(vec["b2"]),
            "Ba_row": row(vec["Ba"]),
            "gvB": np.ascontiguousarray(np.broadcast_to(vec["gv"], (128, E))),
            "bvB": np.ascontiguousarray(np.broadcast_to(vec["Bv"], (128, E))),
            "gaB": np.ascontiguousarray(np.broadcast_to(vec["ga"], (128, E))),
            "gammaB": np.full((128, 1), gamma, np.float32),
            "cmask": cmask, "zpad": zpad,
            "ident_in": ident, "R16_in": R16c, "onesrow_in": onesrow,
            "onescol_in": onescol, "ones16_in": ones16,
        }
        in_maps.append(m)
    return in_maps


def kernel(**inputs):
    in_maps = make_in_maps(inputs)
    nc = bass.Bass("TRN2", target_bir_lowering=False)
    build(nc)
    res = run_bass_kernel_spmd(nc, in_maps, core_ids=list(range(NCORE)))
    outf = np.zeros((B, L, E), np.float32)
    for c in range(NCORE):
        b = c // 4
        t0 = TOK * (c % 4)
        outf[b, t0:t0 + TOK] = res.results[c]["out"]
    return outf
